# version 04 (final)
# baseline (speedup 1.0000x reference)
import numpy as np
import ml_dtypes
from contextlib import ExitStack

import concourse.bass as bass
import concourse.mybir as mybir
from concourse.bass_utils import run_bass_kernel_spmd

F32 = mybir.dt.float32
BF16 = mybir.dt.bfloat16
I32 = mybir.dt.int32
AF = mybir.ActivationFunctionType
ALU = mybir.AluOpType
AX = mybir.AxisListType

T = 4096
D = 1024
NH = 16
HD = 64
NSLOT = 8
NPAIR = 4
E = 6
BEXP = 12.0
EPS = 1e-5

W_SLOT = [128, 256, 512, 896, 1792, 3584, 4096, 4096]
DT_SLOT = [64, 128, 256, 512, 512, 512, 512, 512]
SLOPES = [2.0 ** (-(h + 1) / 2) for h in range(NH)]
MASK_RS = [64, 0, -128, -256, -384]


class Sem:
    def __init__(self, handle):
        self.h = handle
        self.n = 0


class Emitter:
    def __init__(self):
        self.events = []

    def emit(self, engine, fn, waits=()):
        self.events.append(
            (engine, [(s.h, v) for (s, v) in waits if v > 0], fn))

    def replay(self, name, eng):
        last = {}
        for e, waits, fn in self.events:
            if e != name:
                continue
            for h, v in waits:
                if last.get(h, 0) < v:
                    eng.wait_ge(h, v)
                    last[h] = v
            fn(eng)


def chunks_for(slot, t0, dt):
    w = W_SLOT[slot]
    c_lo = max(0, (t0 - w) // 128)
    c_hi = (t0 + dt - 1) // 128
    out = []
    for c in range(c_lo, c_hi + 1):
        masked = (128 * c + 127) > t0
        m = (t0 + dt - 64 - 128 * c) // 64
        out.append((c, masked, m))
    return out


def build_host_consts():
    cst = {}
    cst["ident_bf"] = np.eye(128, dtype=ml_dtypes.bfloat16)
    cst["ident_f32"] = np.eye(128, dtype=np.float32)
    cst["i6_f32"] = np.eye(6, dtype=np.float32)
    cst["ones_bf"] = np.ones((128, 2), dtype=ml_dtypes.bfloat16)
    cst["one_f32"] = np.ones((1, 1), dtype=np.float32)
    cst["eps_col"] = np.full((128, 1), EPS, dtype=np.float32)
    mk = np.zeros((5, 128, 512), dtype=ml_dtypes.bfloat16)
    i = np.arange(128)[:, None]
    j = np.arange(512)[None, :]
    for a, r in enumerate(MASK_RS):
        mk[a] = (i <= j + r).astype(ml_dtypes.bfloat16)
    cst["mask5"] = mk
    ep = np.zeros((7, 128), dtype=np.float32)
    ep[6, 16:32] = 1.0
    for e in range(E):
        ep[e, 32 + 16 * e:32 + 16 * (e + 1)] = 1.0
    cst["e_pat"] = ep
    sel = np.zeros((4, 8, 128), dtype=np.float32)
    for cp in range(4):
        sel[cp, 2 * cp, 0:64] = 1.0
        sel[cp, 2 * cp + 1, 64:128] = 1.0
    cst["sel_pat"] = sel
    return cst


_HOST_CONSTS = build_host_consts()
_MASK_IDX = {r: a for a, r in enumerate(MASK_RS)}


def build_bias_tab(parity):
    tab = np.zeros((128, NSLOT, 64), dtype=np.float32)
    i = np.arange(128, dtype=np.float64)
    for slot in range(NSLOT):
        sl = SLOPES[2 * slot + parity]
        for m in range(64):
            tab[:, slot, m] = (sl * (i - 63 - 64 * m) - BEXP).astype(np.float32)
    return tab


def build_program(taps=(), stop_after=None):
    nc = bass.Bass("TRN2", target_bir_lowering=False,
                   detect_race_conditions=False)
    ap = lambda t: t.ap()
    x_seq = ap(nc.dram_tensor("x_seq", [T, D], F32, kind="ExternalInput"))
    x_res = ap(nc.dram_tensor("x_res", [2048, D], F32, kind="ExternalInput"))
    wq_h = ap(nc.dram_tensor("wq_h", [D, 512], F32, kind="ExternalInput"))
    wk_h = ap(nc.dram_tensor("wk_h", [D, 512], F32, kind="ExternalInput"))
    wv_h = ap(nc.dram_tensor("wv_h", [D, 512], F32, kind="ExternalInput"))
    wp_h = ap(nc.dram_tensor("wp_h", [D, D], F32, kind="ExternalInput"))
    wb_h = ap(nc.dram_tensor("wb_h", [D, D], F32, kind="ExternalInput"))
    acat_h = ap(nc.dram_tensor("acat_h", [D, 128], F32, kind="ExternalInput"))
    bcat_h = ap(nc.dram_tensor("bcat_h", [128, D], F32, kind="ExternalInput"))
    bq_h = ap(nc.dram_tensor("bq_h", [128, 4], F32, kind="ExternalInput"))
    bv_h = ap(nc.dram_tensor("bv_h", [1, 512], F32, kind="ExternalInput"))
    bproj_h = ap(nc.dram_tensor("bproj_h", [128, 8], F32, kind="ExternalInput"))
    bbase_h = ap(nc.dram_tensor("bbase_h", [128, 8], F32, kind="ExternalInput"))
    rb_h = ap(nc.dram_tensor("rb_h", [1, 6], F32, kind="ExternalInput"))
    bias_tab_h = ap(nc.dram_tensor("bias_tab_h", [128, NSLOT * 64], F32, kind="ExternalInput"))
    mask5_h = ap(nc.dram_tensor("mask5_h", [128, 5 * 512], BF16, kind="ExternalInput"))
    identb_h = ap(nc.dram_tensor("identb_h", [128, 128], BF16, kind="ExternalInput"))
    identf_h = ap(nc.dram_tensor("identf_h", [128, 128], F32, kind="ExternalInput"))
    i6_h = ap(nc.dram_tensor("i6_h", [6, 6], F32, kind="ExternalInput"))
    onesb_h = ap(nc.dram_tensor("onesb_h", [128, 2], BF16, kind="ExternalInput"))
    onef_h = ap(nc.dram_tensor("onef_h", [1, 1], F32, kind="ExternalInput"))
    eps_h = ap(nc.dram_tensor("eps_h", [128, 1], F32, kind="ExternalInput"))
    epat_h = ap(nc.dram_tensor("epat_h", [7, 128], F32, kind="ExternalInput"))
    onesr_h = ap(nc.dram_tensor("onesr_h", [1, 512], F32, kind="ExternalInput"))
    sel_h = ap(nc.dram_tensor("sel_h", [8, 4 * 128], F32, kind="ExternalInput"))
    magic_h = ap(nc.dram_tensor("magic_h", [1, 128], I32, kind="ExternalInput"))
    out_half = ap(nc.dram_tensor("out_half", [2048, D], F32, kind="ExternalOutput"))

    y_b = nc.dram_tensor("y_b", [128, 16384], BF16)
    z_b = nc.dram_tensor("z_b", [8, T], F32)
    y_sh = [nc.dram_tensor(f"y_sh{p}", [128, 16384], BF16, addr_space="Shared")
            for p in range(2)]
    z_sh = [nc.dram_tensor(f"z_sh{p}", [8, T], F32, addr_space="Shared")
            for p in range(2)]
    fl_sh = [nc.dram_tensor(f"fl_sh{p}", [1, 128], I32, addr_space="Shared")
             for p in range(2)]

    tap_shapes = {
        "xn1": [128, 8 * 512], "q0": [128, T], "k0": [128, T], "v0": [128, 65],
        "qall": [128, 4 * T], "kall": [128, 4 * T], "vhat": [128, 8 * 32 * 65],
        "xnt0": [128, 1024], "xnf0": [128, 8 * 512],
        "xt0a": [128, 1024], "xt0b": [128, 1024], "cols0": [128, 8],
        "ys0": [64, 64], "zs0": [1, 64],
        "ysend": [128, 16384], "zsend": [8, T], "yrecv": [128, 16384],
        "zrecv": [8, 4096], "ynorm": [128, 8 * 512], "attnp": [128, 8 * 512],
        "x2": [128, 4 * D], "hfm": [128, 512], "wtm": [128, 6],
        "hw": [128, 512], "ffn": [128, 8 * 512],
    }
    tap_outs = {t: ap(nc.dram_tensor("tap_" + t, tap_shapes[t], F32, kind="ExternalOutput"))
                for t in taps}

    ctx0 = ExitStack()
    sb0 = lambda n, s, d=F32: ctx0.enter_context(nc.sbuf_tensor(n, s, d))
    ident_bf = sb0("ident_bf", [128, 128], BF16)
    ident_f = sb0("ident_f", [128, 128], F32)
    i6_sb = sb0("i6_sb", [6, 6], F32)
    ones_bf = sb0("ones_bf", [128, 2], BF16)
    one_f = sb0("one_f", [1, 1], F32)
    eps_col = sb0("eps_col", [128, 1], F32)
    mask5 = sb0("mask5", [128, 5, 512], BF16)
    bias_tab = sb0("bias_tab", [128, NSLOT, 64], F32)
    bq8 = sb0("bq8", [128, 4], F32)
    bq_raw = sb0("bq_raw", [128, 4], F32)
    bv_bc = sb0("bv_bc", [128, 512], F32)
    epat_sb = sb0("epat_sb", [7, 128], F32)
    sel_sb = sb0("sel_sb", [8, 4, 128], F32)
    rb_bc = sb0("rb_bc", [128, 6], F32)
    bproj_c = sb0("bproj_c", [128, 8], F32)
    bbase_c = sb0("bbase_c", [128, 8], F32)
    mg_sb = sb0("mg_sb", [1, 128], I32)

    sctx = ExitStack()
    mks = lambda n: Sem(sctx.enter_context(nc.semaphore(n)))
    d_w, d_c, d_g, d_pull, cc = (mks(n) for n in
                                 ("d_w", "d_c", "d_g", "d_pull", "cc"))
    d_xp = [mks("d_x0"), mks("d_x1")]
    d_yp = [mks("d_y0"), mks("d_y1")]
    d_zp = [mks("d_z0"), mks("d_z1")]
    d_op = [mks("d_o0"), mks("d_o1")]
    v_ln, s_rs, v_nm, s_no, t_tr, v_te = (mks(n) for n in
        ("v_ln", "s_rs", "v_nm", "s_no", "t_tr", "v_te"))
    t_g, s_qe, v_ve = mks("t_g"), mks("s_qe"), mks("v_ve")
    t_qk, s_ex, v_mk, t_av, v_yz = (mks(n) for n in
        ("t_qk", "s_ex", "v_mk", "t_av", "v_yz"))
    bar, v_b, s_b, t_b = mks("bar"), mks("v_b"), mks("s_b"), mks("t_b")
    em = Emitter()

    def replay_block(emx):
        with nc.Block() as blk:
            @blk.sync
            def _(sync):
                emx.replay("sync", sync)

            @blk.gpsimd
            def _(g):
                emx.replay("gpsimd", g)

            @blk.vector
            def _(v):
                emx.replay("vector", v)

            @blk.scalar
            def _(s2):
                emx.replay("scalar", s2)

            @blk.tensor
            def _(t):
                emx.replay("tensor", t)

    def drain_all(emx):
        def fs(sync):
            sync.nop()
        emx.emit("sync", fs, waits=[(d_xp[0], d_xp[0].n), (d_xp[1], d_xp[1].n),
                                    (d_c, d_c.n)])
        def fg(g):
            g.nop()
        emx.emit("gpsimd", fg, waits=[(d_w, d_w.n), (d_g, d_g.n),
                                      (d_yp[0], d_yp[0].n), (d_yp[1], d_yp[1].n),
                                      (d_zp[0], d_zp[0].n), (d_zp[1], d_zp[1].n),
                                      (d_pull, d_pull.n), (cc, cc.n),
                                      (d_op[0], d_op[0].n), (d_op[1], d_op[1].n)])

    ctxA = ExitStack()
    sA = lambda n, s, d=F32: ctxA.enter_context(nc.sbuf_tensor(n, s, d))
    q_pair = [sA(f"q_pair{p}", [128, T], BF16) for p in range(NPAIR)]
    k_pair = [sA(f"k_pair{p}", [128, T], BF16) for p in range(NPAIR)]
    vhat = sA("vhat", [128, NSLOT, 32, 65], BF16)
    z_stage = [sA(f"z_stage{i}", [65, 512], F32) for i in range(2)]
    y_stage = [sA(f"y_stage{i}", [64, 512], BF16) for i in range(2)]
    pbuf = [sA(f"pbuf{i}", [128, 512], BF16) for i in range(2)]
    pmt = [sA(f"pmt{i}", [128, 512], BF16) for i in range(2)]

    ctxA0 = ExitStack()
    sA0 = lambda n, s, d=F32: ctxA0.enter_context(nc.sbuf_tensor(n, s, d))
    x_t = [sA0(f"x_t{i}", [128, D], F32) for i in range(2)]
    sq_bf = sA0("sq_bf", [128, D], BF16)
    xn1_tm = [sA0(f"xn1_tm{i}", [128, D], BF16) for i in range(2)]
    xn1_fm = [sA0(f"xn1_fm{i}", [128, 8, 512], BF16) for i in range(2)]
    wq_bf = sA0("wq_bf", [128, 8, 512], BF16)
    wk_bf = sA0("wk_bf", [128, 8, 512], BF16)
    wv_bf = sA0("wv_bf", [128, 8, 512], BF16)
    scol = [sA0(f"scol{i}", [128, 1], F32) for i in range(2)]
    qcol = [sA0(f"qcol{i}", [128, 1], F32) for i in range(2)]
    mcol = [sA0(f"mcol{i}", [128, 1], F32) for i in range(2)]
    m2col = sA0("m2col", [128, 1], F32)
    vcol = [sA0(f"vcol{i}", [128, 1], F32) for i in range(2)]
    lncol = sA0("lncol", [128, 1], F32)
    rscol = [sA0(f"rscol{i}", [128, 1], F32) for i in range(2)]
    nmcol = [sA0(f"nmcol{i}", [128, 1], F32) for i in range(2)]

    pctxA = ExitStack()
    tr4 = [pctxA.enter_context(nc.psum_tensor(f"tr4_{i}", [128, 128], F32))
           for i in range(4)]
    gen_ps = [pctxA.enter_context(nc.psum_tensor(f"gen_ps{i}", [128, 512], F32)) for i in range(2)]
    ps_y = [pctxA.enter_context(nc.psum_tensor(f"ps_y{i}", [128, 512], F32)) for i in range(2)]

    def load_consts(sync):
        sync.dma_start(out=ident_bf[:], in_=identb_h).then_inc(d_c.h, 16)
        sync.dma_start(out=ident_f[:], in_=identf_h).then_inc(d_c.h, 16)
        sync.dma_start(out=i6_sb[:], in_=i6_h).then_inc(d_c.h, 16)
        sync.dma_start(out=ones_bf[:], in_=onesb_h).then_inc(d_c.h, 16)
        sync.dma_start(out=one_f[:], in_=onef_h).then_inc(d_c.h, 16)
        sync.dma_start(out=eps_col[:], in_=eps_h).then_inc(d_c.h, 16)
        sync.dma_start(out=mask5[:].rearrange("p a c -> p (a c)"), in_=mask5_h).then_inc(d_c.h, 16)
        sync.dma_start(out=bias_tab[:].rearrange("p a c -> p (a c)"), in_=bias_tab_h).then_inc(d_c.h, 16)
        sync.dma_start(out=bq_raw[:], in_=bq_h).then_inc(d_c.h, 16)
        sync.dma_start(out=bv_bc[:], in_=bv_h.broadcast_to((128, 512))).then_inc(d_c.h, 16)
        sync.dma_start(out=epat_sb[:], in_=epat_h).then_inc(d_c.h, 16)
        sync.dma_start(out=sel_sb[:].rearrange("p a c -> p (a c)"), in_=sel_h).then_inc(d_c.h, 16)
        sync.dma_start(out=rb_bc[:], in_=rb_h.broadcast_to((128, 6))).then_inc(d_c.h, 16)
        sync.dma_start(out=bproj_c[:], in_=bproj_h).then_inc(d_c.h, 16)
        sync.dma_start(out=bbase_c[:], in_=bbase_h).then_inc(d_c.h, 16)
        sync.dma_start(out=mg_sb[:], in_=magic_h).then_inc(d_c.h, 16)
    em.emit("sync", load_consts)
    d_c.n += 16 * 16
    NCONST = d_c.n

    def load_wattn(g):
        for kc in range(8):
            g.dma_start(out=wq_bf[:, kc, :], in_=wq_h[128 * kc:128 * (kc + 1), :]).then_inc(d_w.h, 16)
            g.dma_start(out=wk_bf[:, kc, :], in_=wk_h[128 * kc:128 * (kc + 1), :]).then_inc(d_w.h, 16)
            g.dma_start(out=wv_bf[:, kc, :], in_=wv_h[128 * kc:128 * (kc + 1), :]).then_inc(d_w.h, 16)
    em.emit("gpsimd", load_wattn)
    d_w.n += 24 * 16
    NW_A = d_w.n

    def prep_v(v):
        v.tensor_scalar_mul(bq8[:], bq_raw[:], 0.125)
        v.memset(vhat[:, :, :, 64:65], 1.0)
    em.emit("vector", prep_v, waits=[(d_c, NCONST)])

    if stop_after == "consts":
        drain_all(em)
        replay_block(em)
        return nc

    norm_done = {}
    rstd_done = {}
    gen_done = {}
    trA_free = [(v_te, 0)] * 4
    trA_i = [0]
    ps0_free = (s_qe, 0)
    ps1_free = (s_qe, 0)
    n_slabs_a0 = 1 if stop_after in ("a0s1", "a0tile", "a0v", "a0q") else 8
    for slab in range(n_slabs_a0):
        t0s = slab * 512
        for sub in range(4):
            tau = slab * 4 + sub
            bi = tau % 2

            xload_w = [(s_no, norm_done.get(tau - 2, 0))]
            if tau == 2 and "xt0b" in tap_outs:
                xload_w.append((d_g, xt0b_guard))

            def xload(sync, tau=tau, bi=bi):
                sync.dma_start(out=x_t[bi][:], in_=x_seq[128 * tau:128 * (tau + 1), :]
                               ).then_inc(d_xp[bi].h, 16)
            em.emit("sync", xload, waits=xload_w)
            d_xp[bi].n += 16
            if tau == 0 and "xt0a" in tap_outs:
                def tapxt0a(g):
                    g.dma_start(out=tap_outs["xt0a"], in_=x_t[0][:]).then_inc(d_g.h, 16)
                em.emit("gpsimd", tapxt0a, waits=[(d_xp[0], d_xp[0].n)])
                d_g.n += 16
                xt0a_guard = d_g.n

            def vstat(v, bi=bi):
                v.tensor_reduce(scol[bi][:], x_t[bi][:], AX.X, ALU.add)
                v.scalar_tensor_tensor(sq_bf[:], x_t[bi][:], 1.0, x_t[bi][:],
                                       ALU.mult, ALU.mult)
                v.tensor_reduce(qcol[bi][:], sq_bf[:], AX.X, ALU.add)
                v.tensor_scalar_mul(mcol[bi][:], scol[bi][:], 1.0 / D)
                v.drain()
                v.tensor_mul(m2col[:], mcol[bi][:], mcol[bi][:])
                v.drain()
                v.scalar_tensor_tensor(vcol[bi][:], qcol[bi][:], 1.0 / D, m2col[:],
                                       ALU.mult, ALU.subtract)
                v.drain().then_inc(v_ln.h, 1)
            vstat_w = [(d_xp[bi], d_xp[bi].n), (s_rs, rstd_done.get(tau - 2, 0))]
            if tau == 0 and "xt0a" in tap_outs:
                vstat_w.append((d_g, xt0a_guard))
            if tau == 2 and "cols0" in tap_outs:
                vstat_w.append((d_g, cols0_guard))
            em.emit("vector", vstat, waits=vstat_w)
            v_ln.n += 1

            def srstd(s, bi=bi):
                s.activation(lncol[:], vcol[bi][:], AF.Ln, bias=eps_col[:, :])
                s.drain()
                s.activation(rscol[bi][:], lncol[:], AF.Exp, scale=-0.5)
                s.drain().then_inc(s_rs.h, 1)
            em.emit("scalar", srstd, waits=[(v_ln, v_ln.n)])
            s_rs.n += 1
            rstd_done[tau] = s_rs.n

            def vnmr(v, bi=bi):
                v.scalar_tensor_tensor(nmcol[bi][:], mcol[bi][:], -1.0, rscol[bi][:],
                                       ALU.mult, ALU.mult)
                v.drain().then_inc(v_nm.h, 1)
            em.emit("vector", vnmr, waits=[(s_rs, s_rs.n),
                                           (s_no, norm_done.get(tau - 2, 0))])
            v_nm.n += 1
            if tau == 0 and "cols0" in tap_outs:
                def tapcols0(g):
                    for ci, tt in enumerate((scol[0], qcol[0], mcol[0], m2col,
                                             vcol[0], lncol, rscol[0], nmcol[0])):
                        g.dma_start(out=tap_outs["cols0"][:, ci:ci + 1],
                                    in_=tt[:],
                                    allow_slow_non_contiguous=True).then_inc(d_g.h, 16)
                em.emit("gpsimd", tapcols0, waits=[(v_nm, v_nm.n)])
                d_g.n += 8 * 16
                cols0_guard = d_g.n

            snorm_w = [(v_nm, v_nm.n)]
            if tau == 2 and "xnt0" in tap_outs:
                snorm_w.append((d_g, xnt0_guard))

            def snorm(s, bi=bi):
                s.activation(xn1_tm[bi][:], x_t[bi][:], AF.Identity,
                             bias=nmcol[bi][:, :], scale=rscol[bi][:, :])
                s.drain().then_inc(s_no.h, 1)
            em.emit("scalar", snorm, waits=snorm_w)
            s_no.n += 1
            norm_done[tau] = s_no.n
            if tau == 0 and "xnt0" in tap_outs:
                def tapxnt0(g):
                    g.dma_start(out=tap_outs["xnt0"], in_=xn1_tm[0][:]).then_inc(d_g.h, 16)
                em.emit("gpsimd", tapxnt0, waits=[(s_no, s_no.n)])
                d_g.n += 16
                xnt0_guard = d_g.n
            if tau == 0 and "xt0b" in tap_outs:
                def tapxt0b(g):
                    g.dma_start(out=tap_outs["xt0b"], in_=x_t[0][:]).then_inc(d_g.h, 16)
                em.emit("gpsimd", tapxt0b, waits=[(s_no, s_no.n)])
                d_g.n += 16
                xt0b_guard = d_g.n

            for c in range(8):
                bk = trA_i[0] % 4
                trA_i[0] += 1

                def ttr(t, bi=bi, c=c, bk=bk):
                    t.matmul(tr4[bk][:], xn1_tm[bi][:, 128 * c:128 * (c + 1)],
                             ident_bf[:], start=True, stop=True).then_inc(t_tr.h, 1)
                ttr_w = [(s_no, s_no.n), (d_c, NCONST), trA_free[bk],
                         (t_g, gen_done.get(slab - 2, 0))]
                if slab == 2 and sub == 0 and c == 0 and "xnf0" in tap_outs:
                    ttr_w.append((d_g, xnf0_guard))
                em.emit("tensor", ttr, waits=ttr_w)
                t_tr.n += 1
                ttn = t_tr.n

                def vte(v, c=c, sub=sub, slab=slab, bk=bk, ttn=ttn):
                    v.tensor_copy(xn1_fm[slab % 2][:, c, 128 * sub:128 * (sub + 1)],
                                  tr4[bk][:])
                    v.drain().then_inc(v_te.h, 1)
                em.emit("vector", vte, waits=[(t_tr, ttn)])
                v_te.n += 1
                trA_free[bk] = (v_te, v_te.n)

        if slab == 0 and "xnf0" in tap_outs:
            def tapxnf0(g):
                g.dma_start(out=tap_outs["xnf0"],
                            in_=xn1_fm[0][:].rearrange("p c t -> p (c t)")).then_inc(d_g.h, 16)
            em.emit("gpsimd", tapxnf0, waits=[(v_te, v_te.n)])
            d_g.n += 16
            xnf0_guard = d_g.n

        xn = xn1_fm[slab % 2]
        if stop_after in ("a0tile", "a0tile8"):
            gen_done[slab] = t_g.n
            continue
        for p in range(NPAIR if stop_after != "a0v" else 0):
            def qmm(t, p=p, xn=xn):
                for kc in range(8):
                    ins = t.matmul(gen_ps[0][:], wq_bf[:, kc, 128 * p:128 * (p + 1)],
                             xn[:, kc, :], start=(kc == 0), stop=(kc == 7))
                ins.then_inc(t_g.h, 1)
            em.emit("tensor", qmm, waits=[(v_te, v_te.n), (d_w, NW_A), ps0_free])
            t_g.n += 1
            tgn = t_g.n

            def qev(s, p=p, t0s=t0s):
                s.activation(q_pair[p][:, t0s:t0s + 512], gen_ps[0][:], AF.Identity,
                             scale=0.125, bias=bq8[:, p:p + 1])
                s.drain().then_inc(s_qe.h, 1)
            em.emit("scalar", qev, waits=[(t_g, tgn)])
            s_qe.n += 1
            ps0_free = (s_qe, s_qe.n)

            def kmm(t, p=p, xn=xn):
                for kc in range(8):
                    ins = t.matmul(gen_ps[1][:], wk_bf[:, kc, 128 * p:128 * (p + 1)],
                             xn[:, kc, :], start=(kc == 0), stop=(kc == 7))
                ins.then_inc(t_g.h, 1)
            em.emit("tensor", kmm, waits=[ps1_free])
            t_g.n += 1
            tgn = t_g.n

            def kev(s, p=p, t0s=t0s):
                s.copy(out=k_pair[p][:, t0s:t0s + 512], in_=gen_ps[1][:])
                s.drain().then_inc(s_qe.h, 1)
            em.emit("scalar", kev, waits=[(t_g, tgn)])
            s_qe.n += 1
            ps1_free = (s_qe, s_qe.n)

        for sub in range(4 if stop_after != "a0q" else 0):
            tau = slab * 4 + sub

            def vmm(t, sub=sub, xn=xn):
                for kc in range(8):
                    ins = t.matmul(gen_ps[0][:], xn[:, kc, 128 * sub:128 * (sub + 1)],
                             wv_bf[:, kc, :], start=(kc == 0), stop=(kc == 7))
                ins.then_inc(t_g.h, 1)
            em.emit("tensor", vmm, waits=[ps0_free])
            t_g.n += 1
            tgn = t_g.n

            def vev(v, tau=tau):
                v.tensor_tensor(vhat[:, :, tau, 0:64],
                                gen_ps[0][:].rearrange("p (s d) -> p s d", s=8, d=64),
                                bv_bc[:].rearrange("p (s d) -> p s d", s=8, d=64),
                                ALU.add)
                v.drain().then_inc(v_ve.h, 1)
            em.emit("vector", vev, waits=[(t_g, tgn), (d_c, NCONST)])
            v_ve.n += 1
            ps0_free = (v_ve, v_ve.n)
        gen_done[slab] = t_g.n
    A0_SQE, A0_VVE = s_qe.n, v_ve.n

    if "xn1" in tap_outs:
        def tapxn(g):
            g.dma_start(out=tap_outs["xn1"],
                        in_=xn1_fm[1][:].rearrange("p c t -> p (c t)")).then_inc(d_g.h, 16)
        em.emit("gpsimd", tapxn, waits=[(v_te, v_te.n)])
        d_g.n += 16
    if "q0" in tap_outs:
        def tapq(g):
            g.dma_start(out=tap_outs["q0"], in_=q_pair[0][:]).then_inc(d_g.h, 16)
            g.dma_start(out=tap_outs["k0"], in_=k_pair[0][:]).then_inc(d_g.h, 16)
            g.dma_start(out=tap_outs["v0"], in_=vhat[:, 0, 3, :]).then_inc(d_g.h, 16)
        em.emit("gpsimd", tapq, waits=[(s_qe, s_qe.n), (v_ve, v_ve.n)])
        d_g.n += 48
    if "qall" in tap_outs:
        def tapqa(g):
            for p in range(NPAIR):
                g.dma_start(out=tap_outs["qall"][:, p * T:(p + 1) * T],
                            in_=q_pair[p][:]).then_inc(d_g.h, 16)
                g.dma_start(out=tap_outs["kall"][:, p * T:(p + 1) * T],
                            in_=k_pair[p][:]).then_inc(d_g.h, 16)
            g.dma_start(out=tap_outs["vhat"],
                        in_=vhat[:].rearrange("p a b c -> p (a b c)")).then_inc(d_g.h, 16)
        em.emit("gpsimd", tapqa, waits=[(s_qe, s_qe.n), (v_ve, v_ve.n)])
        d_g.n += 9 * 16

    if stop_after in ("a0", "a0s1", "a0tile", "a0tile8", "a0v", "a0q"):
        drain_all(em)
        replay_block(em)
        return nc

    ps_s = gen_ps
    gi = 0
    exp_of = {}
    msk_of = {}
    av_of = {}
    slab_evac = {}
    slab_idx = 0
    for slot in range(NSLOT):
        pr, half = slot >> 1, slot & 1
        base = 64 * half
        cp = slot >> 1
        dt = DT_SLOT[slot]
        for b in range(T // dt):
            t0 = b * dt
            cl = chunks_for(slot, t0, dt)
            for ci, (c, masked, m) in enumerate(cl):
                pb = gi % 2
                qk_waits = [(s_ex, exp_of.get(gi - 2, 0))]
                if gi < 2:
                    qk_waits += [(s_qe, A0_SQE), (v_ve, A0_VVE)]

                def qk(t, pb=pb, base=base, pr=pr, c=c, t0=t0, dt=dt):
                    t.matmul(ps_s[pb][:, 0:dt],
                             k_pair[pr][base:base + 64, 128 * c:128 * (c + 1)],
                             q_pair[pr][base:base + 64, t0:t0 + dt],
                             start=True, stop=True).then_inc(t_qk.h, 1)
                em.emit("tensor", qk, waits=qk_waits)
                t_qk.n += 1

                def ex(s, pb=pb, masked=masked, slot=slot, m=m, dt=dt):
                    dst = pmt[pb] if masked else pbuf[pb]
                    s.activation(dst[:, 0:dt], ps_s[pb][:, 0:dt], AF.Exp,
                                 bias=bias_tab[:, slot, m:m + 1])
                    s.drain().then_inc(s_ex.h, 1)
                em.emit("scalar", ex, waits=[(t_qk, t_qk.n), (d_c, NCONST),
                                             (t_av, av_of.get(gi - 2, 0)),
                                             (v_mk, msk_of.get(gi - 2, 0))])
                s_ex.n += 1
                exp_of[gi] = s_ex.n

                if masked:
                    mi = _MASK_IDX[t0 - 128 * c]

                    def mkf(v, pb=pb, mi=mi, dt=dt):
                        v.tensor_tensor(pbuf[pb][:, 0:dt], pmt[pb][:, 0:dt],
                                        mask5[:, mi, 0:dt], ALU.mult)
                        v.drain().then_inc(v_mk.h, 1)
                    em.emit("vector", mkf, waits=[(s_ex, s_ex.n), (d_c, NCONST),
                                                  (t_av, av_of.get(gi - 2, 0))])
                    v_mk.n += 1
                    msk_of[gi] = v_mk.n
                    av_wait = [(v_mk, v_mk.n)]
                else:
                    av_wait = [(s_ex, s_ex.n)]

                if ci == 0:
                    av_wait.append((v_yz, slab_evac.get(slab_idx - 2, 0)))

                def av(t, pb=pb, slot=slot, c=c, dt=dt, first=(ci == 0),
                       last=(ci == len(cl) - 1), sy=slab_idx % 2):
                    t.matmul(ps_y[sy][0:65, 0:dt], vhat[:, slot, c, :],
                             pbuf[pb][:, 0:dt], start=first, stop=last
                             ).then_inc(t_av.h, 1)
                em.emit("tensor", av, waits=av_wait)
                t_av.n += 1
                av_of[gi] = t_av.n
                gi += 1

            sy = slab_idx % 2
            ccol0 = cp * 4096 + t0
            ob = slab_idx % 2

            yev_w = [(t_av, t_av.n), (d_yp[ob], d_yp[ob].n)]
            if slab_idx == 2 and "ys0" in tap_outs:
                yev_w.append((d_g, ys0_guard))

            def yev(v, sy=sy, ob=ob, dt=dt):
                v.tensor_copy(y_stage[ob][:, 0:dt],
                              ps_y[sy][0:64, 0:dt])
                v.drain().then_inc(v_yz.h, 1)
            em.emit("vector", yev, waits=yev_w)
            v_yz.n += 1
            yn = v_yz.n

            def ydma(g, ob=ob, ccol0=ccol0, dt=dt, half=half):
                g.dma_start(out=y_b.ap()[64 * half:64 * half + 64, ccol0:ccol0 + dt],
                            in_=y_stage[ob][:, 0:dt]).then_inc(d_yp[ob].h, 16)
            em.emit("gpsimd", ydma, waits=[(v_yz, yn)])
            d_yp[ob].n += 16

            zb = slab_idx % 2

            zev_w = [(t_av, t_av.n), (d_zp[zb], d_zp[zb].n)]
            if slab_idx == 2 and "ys0" in tap_outs:
                zev_w.append((d_g, ys0_guard))

            def zev(v, sy=sy, zb=zb, dt=dt):
                v.tensor_copy(z_stage[zb][64:65, 0:dt],
                              ps_y[sy][64:65, 0:dt])
                v.drain().then_inc(v_yz.h, 1)
            em.emit("vector", zev, waits=zev_w)
            v_yz.n += 1
            zn = v_yz.n

            def zdma(g, zb=zb, slot=slot, t0=t0, dt=dt):
                g.dma_start(out=z_b.ap()[slot:slot + 1, t0:t0 + dt],
                            in_=z_stage[zb][64:65, 0:dt]).then_inc(d_zp[zb].h, 16)
            em.emit("gpsimd", zdma, waits=[(v_yz, zn)])
            d_zp[zb].n += 16
            slab_evac[slab_idx] = v_yz.n
            if slab_idx == 0 and "ys0" in tap_outs:
                def tapys0(g):
                    g.dma_start(out=tap_outs["ys0"], in_=y_stage[0][:, 0:64]).then_inc(d_g.h, 16)
                    g.dma_start(out=tap_outs["zs0"], in_=z_stage[0][64:65, 0:64]).then_inc(d_g.h, 16)
                em.emit("gpsimd", tapys0, waits=[(v_yz, v_yz.n)])
                d_g.n += 32
                ys0_guard = d_g.n
            slab_idx += 1
    A1_VYZ = v_yz.n

    if "ysend" in tap_outs:
        def tapy(g):
            g.dma_start(out=tap_outs["ysend"], in_=y_b.ap()[:, :]).then_inc(d_g.h, 16)
            g.dma_start(out=tap_outs["zsend"], in_=z_b.ap()[:, :]).then_inc(d_g.h, 16)
        em.emit("gpsimd", tapy,
                waits=[(v_yz, A1_VYZ), (d_yp[0], d_yp[0].n), (d_yp[1], d_yp[1].n),
                       (d_zp[0], d_zp[0].n), (d_zp[1], d_zp[1].n)])
        d_g.n += 32

    if stop_after == "a1":
        drain_all(em)
        replay_block(em)
        return nc

    STAGE_WAITS = [(d_yp[0], d_yp[0].n), (d_yp[1], d_yp[1].n),
                   (d_zp[0], d_zp[0].n), (d_zp[1], d_zp[1].n)]
    CC0 = cc.n

    def share(g):
        with g.register("r_par_s") as r_par:
            g.reg_mod(r_par, g.partition_id(), 2)
            with g.If_eq(r_par, 0):
                g.dma_start(out=y_sh[0].ap()[:, :], in_=y_b.ap()[:, :]).then_inc(cc.h, 16)
                g.dma_start(out=z_sh[0].ap()[:, :], in_=z_b.ap()[:, :]).then_inc(cc.h, 16)
                g.wait_ge(cc.h, CC0 + 32)
                g.dma_start(out=fl_sh[0].ap()[:, :], in_=mg_sb[:]).then_inc(cc.h, 16)
            with g.Else():
                g.dma_start(out=y_sh[1].ap()[:, :], in_=y_b.ap()[:, :]).then_inc(cc.h, 16)
                g.dma_start(out=z_sh[1].ap()[:, :], in_=z_b.ap()[:, :]).then_inc(cc.h, 16)
                g.wait_ge(cc.h, CC0 + 32)
                g.dma_start(out=fl_sh[1].ap()[:, :], in_=mg_sb[:]).then_inc(cc.h, 16)
    em.emit("gpsimd", share, waits=STAGE_WAITS + [(d_g, d_g.n), (d_c, NCONST)])
    cc.n += 48

    if stop_after == "coll":
        drain_all(em)
        replay_block(em)
        return nc

    def binc(e):
        e.nop()
        e.drain().then_inc(bar.h, 1)
    em.emit("tensor", binc, waits=[(t_av, t_av.n)])
    em.emit("scalar", binc, waits=[(s_ex, s_ex.n)])
    em.emit("vector", binc, waits=[(v_yz, v_yz.n)])
    em.emit("sync", binc, waits=[(d_xp[0], d_xp[0].n), (d_xp[1], d_xp[1].n),
                                 (d_c, NCONST)])
    em.emit("gpsimd", binc, waits=STAGE_WAITS + [(d_g, d_g.n)])
    bar.n = 5

    replay_block(em)
    em = Emitter()
    pctxA.close()
    ctxA0.close()
    ctxA.close()

    ctxB = ExitStack()
    sB = lambda n, s, d=F32: ctxB.enter_context(nc.sbuf_tensor(n, s, d))
    y_recv = sB("y_recv", [128, 16384], BF16)
    y_nslab = sB("y_nslab", [128, 8, 512], BF16)
    z_recv = sB("z_recv", [8, 2 * 2048], F32)
    zinv_sb = sB("zinv_sb", [128, 512], F32)
    wp_bf = sB("wp_bf", [128, 8, D], BF16)
    wb_all = sB("wb_all", [128, 8, D], BF16)
    wbt_bf = sB("wbt_bf", [128, 8, 8, 128], BF16)
    acat_bf = sB("acat_bf", [128, 8, 128], BF16)
    bcat_bf = sB("bcat_bf", [128, D], BF16)
    x_sub = [sB(f"x_sub{i}", [128, D], F32) for i in range(2)]
    x2_tm = sB("x2_tm", [128, 4, D], F32)
    attnp_fm = sB("attnp_fm", [128, 8, 512], BF16)
    ffn_fm = attnp_fm
    xn2_tm = sB("xn2_tm", [128, D], BF16)
    xn2_fm = sB("xn2_fm", [128, 8, 512], BF16)
    asq_bf = sB("asq_bf", [128, 512], BF16)
    sqb_bf = sB("sqb_bf", [128, D], BF16)
    h_fm = sB("h_fm", [128, 512], F32)
    hw_bf = sB("hw_bf", [128, 512], BF16)
    w_fm7 = sB("w_fm7", [7, 512], F32)
    ssum_row = sB("ssum_row", [1, 512], F32)
    ssq_row = sB("ssq_row", [1, 512], F32)
    g0 = sB("g0", [128, 6], F32)
    gmx = sB("gmx", [128, 1], F32)
    gmxn = sB("gmxn", [128, 1], F32)
    ge = sB("ge", [128, 6], F32)
    ges = sB("ges", [128, 1], F32)
    gesr = sB("gesr", [128, 1], F32)
    gate = sB("gate", [128, 6], F32)
    msk1 = sB("msk1", [128, 6], F32)
    gg2 = sB("gg2", [128, 6], F32)
    gmx2 = sB("gmx2", [128, 1], F32)
    gmx3 = sB("gmx3", [128, 1], F32)
    msk2 = sB("msk2", [128, 6], F32)
    wun = sB("wun", [128, 6], F32)
    wsum = sB("wsum", [128, 1], F32)
    wsum2 = sB("wsum2", [128, 1], F32)
    wsr = sB("wsr", [128, 1], F32)
    w_tm = sB("w_tm", [128, 6], F32)
    mrow = sB("mrow", [1, 512], F32)
    m2row = sB("m2row", [1, 512], F32)
    vrow = sB("vrow", [1, 512], F32)
    lnrow = sB("lnrow", [1, 512], F32)
    rsrow = sB("rsrow", [1, 512], F32)
    rn_cols = sB("rn_cols", [128, 4, 2], F32)
    m_cols = sB("m_cols", [128, 4], F32)
    ln_f32 = sB("ln_f32", [128, D], F32)
    out_tm = [sB(f"out_tm{i}", [128, D], F32) for i in range(2)]
    b_scol = sB("b_scol", [128, 1], F32)
    b_qcol = sB("b_qcol", [128, 1], F32)
    b_mcol = sB("b_mcol", [128, 1], F32)
    b_m2 = sB("b_m2", [128, 1], F32)
    b_vcol = sB("b_vcol", [128, 1], F32)
    b_ln = sB("b_ln", [128, 1], F32)
    b_rs = sB("b_rs", [128, 1], F32)
    b_nm = sB("b_nm", [128, 1], F32)

    pctxB = ExitStack()
    ps_fm = [pctxB.enter_context(nc.psum_tensor(f"ps_fm{i}", [128, 512], F32)) for i in range(2)]
    ps_h = pctxB.enter_context(nc.psum_tensor("ps_h", [128, 512], F32))
    ps_misc = pctxB.enter_context(nc.psum_tensor("ps_misc", [128, 512], F32))
    trb = [pctxB.enter_context(nc.psum_tensor(f"trb{i}", [128, 128], F32))
           for i in range(4)]
    trb_free = [(v_b, 0)] * 4
    trb_i = [0]

    def trb_next():
        b = trb_i[0] % 4
        trb_i[0] += 1
        return b

    def loadB(g):
        for kc in range(8):
            g.dma_start(out=wp_bf[:, kc, :], in_=wp_h[128 * kc:128 * (kc + 1), :]).then_inc(d_w.h, 16)
        for kc in range(8):
            g.dma_start(out=acat_bf[:, kc, :], in_=acat_h[128 * kc:128 * (kc + 1), :]).then_inc(d_w.h, 16)
        g.dma_start(out=bcat_bf[:], in_=bcat_h).then_inc(d_w.h, 16)
        for ot in range(8):
            g.dma_start(out=wb_all[:, ot, :], in_=wb_h[128 * ot:128 * (ot + 1), :]).then_inc(d_w.h, 16)
        g.dma_start(out=w_fm7[6:7, :], in_=onesr_h).then_inc(d_w.h, 16)
    em.emit("gpsimd", loadB, waits=[(bar, 5)])
    d_w.n += 26 * 16
    WB_DW = d_w.n

    for ot in range(8):
        for ic in range(8):
            bk = trb_next()

            def wbt_mm(t, ot=ot, ic=ic, bk=bk):
                t.matmul(trb[bk][:], wb_all[:, ot, 128 * ic:128 * (ic + 1)],
                         ident_bf[:], start=True, stop=True).then_inc(t_b.h, 1)
            em.emit("tensor", wbt_mm, waits=[(d_w, WB_DW), (bar, 5), trb_free[bk]])
            t_b.n += 1
            tbn = t_b.n

            def wbt_ev(v, ot=ot, ic=ic, bk=bk, tbn=tbn):
                v.tensor_copy(wbt_bf[:, ic, ot, :], trb[bk][:])
                v.drain().then_inc(v_b.h, 1)
            em.emit("vector", wbt_ev, waits=[(t_b, tbn)])
            v_b.n += 1
            trb_free[bk] = (v_b, v_b.n)
    WBT_VB = v_b.n

    if stop_after == "wbt":
        drain_all(em)
        replay_block(em)
        pctxB.close()
        ctxB.close()
        sctx.close()
        ctx0.close()
        return nc

    def pull(g):
        with g.register("rm1") as rm1, g.register("rm2") as rm2, \
             g.register("rv1") as rv1, g.register("rv2") as rv2, \
             g.register("cond") as cond:
            g.reg_load(rm1, mg_sb[0:1, 0:1])
            g.reg_load(rm2, mg_sb[0:1, 1:2])
            for p in range(2):
                g.reg_mov(cond, 1)
                with g.While(cond):
                    g.reg_load(rv1, fl_sh[p].ap()[0:1, 0:1])
                    g.reg_load(rv2, fl_sh[p].ap()[0:1, 1:2])
                    g.reg_alu(rv1, rv1, rm1, ALU.not_equal)
                    g.reg_alu(rv2, rv2, rm2, ALU.not_equal)
                    g.reg_alu(cond, rv1, rv2, ALU.logical_or)
        with g.register("r_par") as r_par:
            g.reg_mod(r_par, g.partition_id(), 2)
            with g.If_eq(r_par, 0):
                for k2 in range(2):
                    g.dma_start(
                        out=y_recv[:, k2 * 8192:(k2 + 1) * 8192].rearrange(
                            "p (c t) -> p c t", c=4, t=2048),
                        in_=y_sh[k2].ap().rearrange("p (c t) -> p c t", c=4, t=4096)[:, :, 0:2048],
                    ).then_inc(d_pull.h, 16)
                    g.dma_start(out=z_recv[:, k2 * 2048:(k2 + 1) * 2048],
                                in_=z_sh[k2].ap()[:, 0:2048]).then_inc(d_pull.h, 16)
            with g.Else():
                for k2 in range(2):
                    g.dma_start(
                        out=y_recv[:, k2 * 8192:(k2 + 1) * 8192].rearrange(
                            "p (c t) -> p c t", c=4, t=2048),
                        in_=y_sh[k2].ap().rearrange("p (c t) -> p c t", c=4, t=4096)[:, :, 2048:4096],
                    ).then_inc(d_pull.h, 16)
                    g.dma_start(out=z_recv[:, k2 * 2048:(k2 + 1) * 2048],
                                in_=z_sh[k2].ap()[:, 2048:4096]).then_inc(d_pull.h, 16)
    em.emit("gpsimd", pull, waits=[(bar, 5)])
    d_pull.n += 64

    if "yrecv" in tap_outs:
        def tapyr(g):
            g.dma_start(out=tap_outs["yrecv"], in_=y_recv[:]).then_inc(d_g.h, 16)
            g.dma_start(out=tap_outs["zrecv"], in_=z_recv[:]).then_inc(d_g.h, 16)
        em.emit("gpsimd", tapyr, waits=[(d_pull, 64)])
        d_g.n += 32

    if stop_after == "pull":
        drain_all(em)
        replay_block(em)
        pctxB.close()
        ctxB.close()
        sctx.close()
        ctx0.close()
        return nc

    def fm_ln_stats(src_fm, ev_sems):
        stat_t = [0]
        for oc in range(8):
            sq_w = [ev_sems[oc]]
            if oc > 0:
                sq_w.append((t_b, stat_t[0]))

            def sqp(v, oc=oc, src_fm=src_fm):
                v.scalar_tensor_tensor(asq_bf[:], src_fm[:, oc, :], 1.0,
                                       src_fm[:, oc, :], ALU.mult, ALU.mult
                                       )
                v.drain().then_inc(v_b.h, 1)
            em.emit("vector", sqp, waits=sq_w)
            v_b.n += 1
            vbn = v_b.n

            def stmm(t, oc=oc, src_fm=src_fm):
                t.matmul(ps_misc[0:1, 0:512], ones_bf[:, 0:1], src_fm[:, oc, :],
                         start=(oc == 0), stop=(oc == 7))
                t.matmul(ps_h[0:1, 0:512], ones_bf[:, 0:1], asq_bf[:],
                         start=(oc == 0), stop=(oc == 7)).then_inc(t_b.h, 1)
            em.emit("tensor", stmm, waits=[(v_b, vbn)])
            t_b.n += 1
            stat_t[0] = t_b.n

    def row_stats_to_cols():
        def revac(v):
            v.tensor_copy(ssum_row[:], ps_misc[0:1, 0:512])
            v.tensor_copy(ssq_row[:], ps_h[0:1, 0:512])
            v.drain().then_inc(v_b.h, 1)
        em.emit("vector", revac, waits=[(t_b, t_b.n)])
        v_b.n += 1

        def rmath_v(v):
            v.tensor_scalar_mul(mrow[:], ssum_row[:], 1.0 / D)
            v.drain()
            v.tensor_mul(m2row[:], mrow[:], mrow[:])
            v.drain()
            v.scalar_tensor_tensor(vrow[:], ssq_row[:], 1.0 / D, m2row[:],
                                   ALU.mult, ALU.subtract)
            v.drain().then_inc(v_b.h, 1)
        em.emit("vector", rmath_v)
        v_b.n += 1

        def rmath_s(s):
            s.activation(lnrow[:], vrow[:], AF.Ln, bias=eps_col[0:1, :])
            s.drain()
            s.activation(rsrow[:], lnrow[:], AF.Exp, scale=-0.5)
            s.drain().then_inc(s_b.h, 1)
        em.emit("scalar", rmath_s, waits=[(v_b, v_b.n)])
        s_b.n += 1
        sbn = s_b.n

        for sub in range(4):
            bkm = trb_next()
            bkr = trb_next()

            def mcmm(t, sub=sub, bkm=bkm, bkr=bkr, sbn=sbn):
                t.matmul(trb[bkm][:, 0:1], mrow[0:1, 128 * sub:128 * (sub + 1)],
                         one_f[:], start=True, stop=True)
                t.matmul(trb[bkr][:, 0:1], rsrow[0:1, 128 * sub:128 * (sub + 1)],
                         one_f[:], start=True, stop=True).then_inc(t_b.h, 1)
            em.emit("tensor", mcmm, waits=[(s_b, sbn), trb_free[bkm],
                                           trb_free[bkr]])
            t_b.n += 1
            tbn = t_b.n

            def cevac(v, sub=sub, bkm=bkm, bkr=bkr, tbn=tbn):
                v.tensor_copy(m_cols[:, sub:sub + 1], trb[bkm][:, 0:1])
                v.tensor_copy(rn_cols[:, sub, 0:1], trb[bkr][:, 0:1])
                v.drain()
                v.scalar_tensor_tensor(rn_cols[:, sub, 1:2], m_cols[:, sub:sub + 1],
                                       -1.0, rn_cols[:, sub, 0:1],
                                       ALU.mult, ALU.mult)
                v.drain().then_inc(v_b.h, 1)
            em.emit("vector", cevac, waits=[(t_b, tbn)])
            v_b.n += 1
            trb_free[bkm] = (v_b, v_b.n)
            trb_free[bkr] = (v_b, v_b.n)

    xsub_free = [(v_b, 0), (v_b, 0)]
    ln_read = (v_b, 0)
    xn2tm_read = (t_b, 0)
    router_read = (s_b, 0)
    pmm_prev = 0
    for sl in range(1 if stop_after == "slab1" else 4):
        tcol0 = 512 * sl
        for kc in range(8):
            k2, cpi = kc >> 2, kc & 3

            def zmm(t, k2=k2, cpi=cpi, tcol0=tcol0):
                t.matmul(ps_misc[:, 0:512], sel_sb[:, cpi, :],
                         z_recv[:, k2 * 2048 + tcol0:k2 * 2048 + tcol0 + 512],
                         start=True, stop=True).then_inc(t_b.h, 1)
            em.emit("tensor", zmm, waits=[(d_pull, 64), (v_b, v_b.n),
                                          (v_b, WBT_VB)])
            t_b.n += 1
            tbn = t_b.n

            def zn(v, kc=kc, tcol0=tcol0, tbn=tbn):
                col = kc * 2048 + tcol0
                v.reciprocal(zinv_sb[:], ps_misc[:, 0:512])
                v.drain()
                v.tensor_tensor(y_nslab[:, kc, :], y_recv[:, col:col + 512],
                                zinv_sb[:], ALU.mult)
                v.drain().then_inc(v_b.h, 1)
            em.emit("vector", zn, waits=[(t_b, tbn), (t_b, pmm_prev)])
            v_b.n += 1
        YN_VB = v_b.n
        if "ynorm" in tap_outs and sl == 0:
            def tapyn(g):
                g.dma_start(out=tap_outs["ynorm"],
                            in_=y_nslab[:].rearrange("p c t -> p (c t)")).then_inc(d_g.h, 16)
            em.emit("gpsimd", tapyn, waits=[(v_b, YN_VB)])
            d_g.n += 16

        ev_sems = []
        for oc in range(8):
            w = [(s_b, s_b.n), (v_b, YN_VB)]

            def pmm(t, oc=oc):
                for kc in range(8):
                    ins = t.matmul(ps_fm[oc % 2][:, 0:512],
                                   wp_bf[:, kc, 128 * oc:128 * (oc + 1)],
                                   y_nslab[:, kc, :],
                                   start=(kc == 0), stop=(kc == 7))
                ins.then_inc(t_b.h, 1)
            em.emit("tensor", pmm, waits=w)
            t_b.n += 1
            tbn = t_b.n
            if oc == 7:
                pmm_prev = t_b.n

            def pev(s, oc=oc, tbn=tbn):
                s.activation(attnp_fm[:, oc, :], ps_fm[oc % 2][:, 0:512], AF.Identity,
                             bias=bproj_c[:, oc:oc + 1])
                s.drain().then_inc(s_b.h, 1)
            em.emit("scalar", pev, waits=[(t_b, tbn)])
            s_b.n += 1
            ev_sems.append((s_b, s_b.n))
        fm_ln_stats(attnp_fm, ev_sems)
        row_stats_to_cols()

        for sub in range(4):
            bi2 = sub % 2

            def xl(sync, sl=sl, sub=sub, bi2=bi2):
                sync.dma_start(out=x_sub[bi2][:],
                               in_=x_res[512 * sl + 128 * sub:512 * sl + 128 * (sub + 1), :]
                               ).then_inc(d_xp[bi2].h, 16)
            em.emit("sync", xl, waits=[(bar, 5), xsub_free[bi2]])
            d_xp[bi2].n += 16
            dxn = (d_xp[bi2], d_xp[bi2].n)

            aln_last = 0
            for c in range(8):
                bk = trb_next()

                def atr(t, c=c, sub=sub, bk=bk):
                    t.matmul(trb[bk][:], attnp_fm[:, c, 128 * sub:128 * (sub + 1)],
                             ident_bf[:], start=True, stop=True).then_inc(t_b.h, 1)
                em.emit("tensor", atr, waits=[(v_b, v_b.n), (s_b, s_b.n),
                                              trb_free[bk]])
                t_b.n += 1
                tbn = t_b.n

                def aln(s, c=c, sub=sub, bk=bk, tbn=tbn):
                    if c == 7:
                        s.activation(ln_f32[:, 128 * c:128 * (c + 1)], trb[bk][:],
                                     AF.Identity, scale=rn_cols[:, sub, 0:1],
                                     bias=rn_cols[:, sub, 1:2])
                        s.drain().then_inc(s_b.h, 1)
                    else:
                        s.activation(ln_f32[:, 128 * c:128 * (c + 1)], trb[bk][:],
                                     AF.Identity, scale=rn_cols[:, sub, 0:1],
                                     bias=rn_cols[:, sub, 1:2]).then_inc(s_b.h, 1)
                em.emit("scalar", aln, waits=[(t_b, tbn), ln_read])
                s_b.n += 1
                trb_free[bk] = (s_b, s_b.n)
                aln_last = s_b.n

            def ares(v, sub=sub, aln_last=aln_last):
                v.tensor_add(x2_tm[:, sub, :], ln_f32[:], x_sub[sub % 2][:]
                             )
                v.drain().then_inc(v_b.h, 1)
            em.emit("vector", ares, waits=[(s_b, aln_last), dxn])
            v_b.n += 1
            ln_read = (v_b, v_b.n)
            xsub_free[sub % 2] = (v_b, v_b.n)

            def l2v(v, sub=sub):
                v.tensor_reduce(b_scol[:], x2_tm[:, sub, :], AX.X, ALU.add)
                v.scalar_tensor_tensor(sqb_bf[:], x2_tm[:, sub, :], 1.0,
                                       x2_tm[:, sub, :], ALU.mult, ALU.mult)
                v.tensor_reduce(b_qcol[:], sqb_bf[:], AX.X, ALU.add)
                v.tensor_scalar_mul(b_mcol[:], b_scol[:], 1.0 / D)
                v.drain()
                v.tensor_mul(b_m2[:], b_mcol[:], b_mcol[:])
                v.drain()
                v.scalar_tensor_tensor(b_vcol[:], b_qcol[:], 1.0 / D, b_m2[:],
                                       ALU.mult, ALU.subtract)
                v.drain().then_inc(v_b.h, 1)
            em.emit("vector", l2v)
            v_b.n += 1

            def l2s(s):
                s.activation(b_ln[:], b_vcol[:], AF.Ln, bias=eps_col[:, :])
                s.drain()
                s.activation(b_rs[:], b_ln[:], AF.Exp, scale=-0.5)
                s.drain().then_inc(s_b.h, 1)
            em.emit("scalar", l2s, waits=[(v_b, v_b.n)])
            s_b.n += 1

            def l2n(v):
                v.scalar_tensor_tensor(b_nm[:], b_mcol[:], -1.0, b_rs[:],
                                       ALU.mult, ALU.mult)
                v.drain().then_inc(v_b.h, 1)
            em.emit("vector", l2n, waits=[(s_b, s_b.n)])
            v_b.n += 1

            def l2no(s, sub=sub):
                s.activation(xn2_tm[:], x2_tm[:, sub, :], AF.Identity,
                             bias=b_nm[:, :], scale=b_rs[:, :])
                s.drain().then_inc(s_b.h, 1)
            em.emit("scalar", l2no, waits=[(v_b, v_b.n), xn2tm_read])
            s_b.n += 1
            sbn2 = s_b.n

            for c in range(8):
                bk = trb_next()

                def x2tr(t, c=c, bk=bk, sbn2=sbn2):
                    t.matmul(trb[bk][:], xn2_tm[:, 128 * c:128 * (c + 1)],
                             ident_bf[:], start=True, stop=True).then_inc(t_b.h, 1)
                em.emit("tensor", x2tr, waits=[(s_b, sbn2), trb_free[bk]])
                t_b.n += 1
                if c == 7:
                    xn2tm_read = (t_b, t_b.n)
                tbn3 = t_b.n

                def x2te(v, c=c, sub=sub, bk=bk, tbn3=tbn3):
                    v.tensor_copy(xn2_fm[:, c, 128 * sub:128 * (sub + 1)],
                                  trb[bk][:])
                    v.drain().then_inc(v_b.h, 1)
                em.emit("vector", x2te, waits=[(t_b, tbn3)])
                v_b.n += 1
                trb_free[bk] = (v_b, v_b.n)
        XN2_VB = v_b.n

        def mm1(t):
            for ic in range(8):
                ins = t.matmul(ps_h[:, 0:512], acat_bf[:, ic, :], xn2_fm[:, ic, :],
                               start=(ic == 0), stop=(ic == 7))
            ins.then_inc(t_b.h, 1)
        em.emit("tensor", mm1, waits=[(v_b, XN2_VB)])
        t_b.n += 1
        tbn4 = t_b.n

        def hev(v, tbn4=tbn4):
            v.tensor_copy(h_fm[:], ps_h[:, 0:512])
            v.drain().then_inc(v_b.h, 1)
        em.emit("vector", hev, waits=[(t_b, tbn4)])
        v_b.n += 1
        HF_VB = v_b.n

        for sub in range(4):
            bk6 = trb_next()

            def lgmm(t, sub=sub, bk6=bk6):
                t.matmul(trb[bk6][:, 0:6], h_fm[0:6, 128 * sub:128 * (sub + 1)],
                         i6_sb[:], start=True, stop=True).then_inc(t_b.h, 1)
            em.emit("tensor", lgmm, waits=[(v_b, HF_VB), trb_free[bk6]])
            t_b.n += 1
            tbn5 = t_b.n

            def router(v, tbn5=tbn5, bk6=bk6):
                v.tensor_add(g0[:], trb[bk6][:, 0:6], rb_bc[:])
                v.drain()
                v.tensor_reduce(gmx[:], g0[:], AX.X, ALU.max)
                v.drain()
                v.tensor_scalar_mul(gmxn[:], gmx[:], -1.0)
                v.drain().then_inc(v_b.h, 1)
            em.emit("vector", router, waits=[(t_b, tbn5), router_read])
            v_b.n += 1
            trb_free[bk6] = (v_b, v_b.n)

            def rexp(s):
                s.activation(ge[:], g0[:], AF.Exp, bias=gmxn[:, :])
                s.drain().then_inc(s_b.h, 1)
            em.emit("scalar", rexp, waits=[(v_b, v_b.n)])
            s_b.n += 1
            router_read = (s_b, s_b.n)

            def top2(v):
                v.tensor_reduce(ges[:], ge[:], AX.X, ALU.add)
                v.drain()
                v.reciprocal(gesr[:], ges[:])
                v.drain()
                v.tensor_scalar(gate[:], ge[:], gesr[:, :], None, ALU.mult)
                v.drain()
                v.tensor_reduce(gmx2[:], gate[:], AX.X, ALU.max)
                v.drain()
                v.tensor_scalar(msk1[:], gate[:], gmx2[:, :], None, ALU.is_ge)
                v.drain()
                v.scalar_tensor_tensor(gg2[:], msk1[:], -1e9, gate[:], ALU.mult, ALU.add)
                v.drain()
                v.tensor_reduce(gmx3[:], gg2[:], AX.X, ALU.max)
                v.drain()
                v.tensor_scalar(msk2[:], gate[:], gmx3[:, :], None, ALU.is_ge)
                v.drain()
                v.tensor_mul(wun[:], gate[:], msk2[:])
                v.drain()
                v.tensor_reduce(wsum[:], wun[:], AX.X, ALU.add)
                v.drain()
                v.tensor_scalar(wsum2[:], wsum[:], 1e-6, None, ALU.add)
                v.drain()
                v.reciprocal(wsr[:], wsum2[:])
                v.drain()
                v.tensor_scalar(w_tm[:], wun[:], wsr[:, :], None, ALU.mult
                                )
                v.drain().then_inc(v_b.h, 1)
            em.emit("vector", top2, waits=[(s_b, s_b.n)])
            v_b.n += 1
            vbn6 = v_b.n

            bk7 = trb_next()

            def wtr(t, bk7=bk7, vbn6=vbn6):
                t.matmul(trb[bk7][0:6, 0:128], w_tm[:], ident_f[:],
                         start=True, stop=True).then_inc(t_b.h, 1)
            em.emit("tensor", wtr, waits=[(v_b, vbn6), trb_free[bk7]])
            t_b.n += 1
            wtr_last = t_b.n

            def wfev_sub(v, sub=sub, bk7=bk7, wtr_last=wtr_last):
                v.tensor_copy(w_fm7[0:6, 128 * sub:128 * (sub + 1)],
                              trb[bk7][0:6, 0:128])
                v.drain().then_inc(v_b.h, 1)
            em.emit("vector", wfev_sub, waits=[(t_b, wtr_last)])
            v_b.n += 1
            trb_free[bk7] = (v_b, v_b.n)
            if "wtm" in tap_outs and sl == 0 and sub == 0:
                def tapw(g):
                    g.dma_start(out=tap_outs["wtm"], in_=w_tm[:]).then_inc(d_g.h, 16)
                em.emit("gpsimd", tapw, waits=[(v_b, vbn6)])
                d_g.n += 16

        vbn7 = v_b.n

        def wbc(t, vbn7=vbn7):
            t.matmul(ps_misc[:, 0:512], epat_sb[:], w_fm7[:],
                     start=True, stop=True).then_inc(t_b.h, 1)
        em.emit("tensor", wbc, waits=[(v_b, vbn7), (d_w, d_w.n)])
        t_b.n += 1
        tbn8 = t_b.n

        def hwm(v, tbn8=tbn8):
            v.tensor_tensor(hw_bf[:], h_fm[:], ps_misc[:, 0:512], ALU.mult
                            )
            v.drain().then_inc(v_b.h, 1)
        em.emit("vector", hwm, waits=[(t_b, tbn8)])
        v_b.n += 1
        HW_VB = v_b.n

        ev_sems = []
        for oc in range(8):
            def fmm(t, oc=oc):
                for ic in range(8):
                    t.matmul(ps_fm[oc % 2][:, 0:512], wbt_bf[:, ic, oc, :],
                             xn2_fm[:, ic, :], start=(ic == 0), stop=False)
                t.matmul(ps_fm[oc % 2][:, 0:512],
                         bcat_bf[:, 128 * oc:128 * (oc + 1)], hw_bf[:],
                         start=False, stop=True).then_inc(t_b.h, 1)
            em.emit("tensor", fmm, waits=[(v_b, HW_VB), (s_b, s_b.n)])
            t_b.n += 1
            tbn9 = t_b.n

            def fev(s, oc=oc, tbn9=tbn9):
                s.activation(ffn_fm[:, oc, :], ps_fm[oc % 2][:, 0:512], AF.Identity,
                             bias=bbase_c[:, oc:oc + 1])
                s.drain().then_inc(s_b.h, 1)
            em.emit("scalar", fev, waits=[(t_b, tbn9)])
            s_b.n += 1
            ev_sems.append((s_b, s_b.n))
        fm_ln_stats(ffn_fm, ev_sems)
        row_stats_to_cols()

        for sub in range(4):
            fln_last = 0
            for c in range(8):
                bk = trb_next()

                def ftr(t, c=c, sub=sub, bk=bk):
                    t.matmul(trb[bk][:], ffn_fm[:, c, 128 * sub:128 * (sub + 1)],
                             ident_bf[:], start=True, stop=True).then_inc(t_b.h, 1)
                em.emit("tensor", ftr, waits=[(v_b, v_b.n), (s_b, s_b.n),
                                              trb_free[bk]])
                t_b.n += 1
                tbn10 = t_b.n

                def fln(s, c=c, sub=sub, bk=bk, tbn10=tbn10):
                    if c == 7:
                        s.activation(ln_f32[:, 128 * c:128 * (c + 1)], trb[bk][:],
                                     AF.Identity, scale=rn_cols[:, sub, 0:1],
                                     bias=rn_cols[:, sub, 1:2])
                        s.drain().then_inc(s_b.h, 1)
                    else:
                        s.activation(ln_f32[:, 128 * c:128 * (c + 1)], trb[bk][:],
                                     AF.Identity, scale=rn_cols[:, sub, 0:1],
                                     bias=rn_cols[:, sub, 1:2]).then_inc(s_b.h, 1)
                em.emit("scalar", fln, waits=[(t_b, tbn10), ln_read])
                s_b.n += 1
                trb_free[bk] = (s_b, s_b.n)
                fln_last = s_b.n

            def fres(v, sub=sub, fln_last=fln_last):
                v.tensor_add(out_tm[sub % 2][:], ln_f32[:], x2_tm[:, sub, :]
                             )
                v.drain().then_inc(v_b.h, 1)
            em.emit("vector", fres, waits=[(s_b, fln_last),
                                           (d_op[sub % 2], d_op[sub % 2].n)])
            v_b.n += 1
            ln_read = (v_b, v_b.n)
            vbn12 = v_b.n

            def ow(sync, sl=sl, sub=sub, vbn12=vbn12):
                sync.dma_start(
                    out=out_half[512 * sl + 128 * sub:512 * sl + 128 * (sub + 1), :],
                    in_=out_tm[sub % 2][:]).then_inc(d_op[sub % 2].h, 16)
            em.emit("sync", ow, waits=[(v_b, vbn12)])
            d_op[sub % 2].n += 16

        if "attnp" in tap_outs and sl == 0:
            def tapap(g):
                g.dma_start(out=tap_outs["attnp"],
                            in_=attnp_fm[:].rearrange("p c t -> p (c t)")).then_inc(d_g.h, 16)
                g.dma_start(out=tap_outs["x2"],
                            in_=x2_tm[:].rearrange("p c t -> p (c t)")).then_inc(d_g.h, 16)
                g.dma_start(out=tap_outs["hfm"], in_=h_fm[:]).then_inc(d_g.h, 16)
                g.dma_start(out=tap_outs["hw"], in_=hw_bf[:]).then_inc(d_g.h, 16)
                g.dma_start(out=tap_outs["ffn"],
                            in_=ffn_fm[:].rearrange("p c t -> p (c t)")).then_inc(d_g.h, 16)
            em.emit("gpsimd", tapap, waits=[(v_b, v_b.n), (s_b, s_b.n)])
            d_g.n += 5 * 16

    def fin(sync):
        sync.nop()
    em.emit("sync", fin, waits=[(d_op[0], d_op[0].n), (d_op[1], d_op[1].n)])
    if tap_outs:
        def fing(g):
            g.nop()
        em.emit("gpsimd", fing, waits=[(d_g, d_g.n)])

    replay_block(em)
    pctxB.close()
    ctxB.close()
    sctx.close()
    ctx0.close()
    return nc


def _prep_inputs(inputs):
    x = np.ascontiguousarray(np.asarray(inputs["L0_attn_x"], dtype=np.float32))
    w_attn = np.asarray(inputs["L0_attn_w_attn"], dtype=np.float32)
    b_attn = np.asarray(inputs["L0_attn_b_attn"], dtype=np.float32)
    w_proj = np.asarray(inputs["L0_attn_w_proj"], dtype=np.float32)
    b_proj = np.asarray(inputs["L0_attn_b_proj"], dtype=np.float32)
    base_W = np.asarray(inputs["L0_ffn_base_W"], dtype=np.float32)
    base_b = np.asarray(inputs["L0_ffn_base_b"], dtype=np.float32)
    shared_A = np.asarray(inputs["L0_ffn_shared_A"], dtype=np.float32)
    shared_B = np.asarray(inputs["L0_ffn_shared_B"], dtype=np.float32)
    routed_A = np.asarray(inputs["L0_ffn_routed_A"], dtype=np.float32)
    routed_B = np.asarray(inputs["L0_ffn_routed_B"], dtype=np.float32)
    router_W = np.asarray(inputs["L0_ffn_router_W"], dtype=np.float32)
    router_b = np.asarray(inputs["L0_ffn_router_b"], dtype=np.float32)

    cst = _HOST_CONSTS
    acat = np.zeros((D, 128), dtype=np.float32)
    acat[:, 0:6] = router_W.T
    acat[:, 16:32] = shared_A.T
    acat[:, 32:128] = routed_A.reshape(96, D).T
    bcat = np.zeros((128, D), dtype=np.float32)
    bcat[16:32, :] = shared_B.T
    bcat[32:128, :] = np.transpose(routed_B, (0, 2, 1)).reshape(96, D)

    common = dict(
        wb_h=np.ascontiguousarray(base_W),
        acat_h=np.ascontiguousarray(acat),
        bcat_h=np.ascontiguousarray(bcat),
        rb_h=np.ascontiguousarray(router_b.reshape(1, 6)),
        bbase_h=np.ascontiguousarray(base_b.reshape(8, 128).T),
        mask5_h=np.ascontiguousarray(
            cst["mask5"].transpose(1, 0, 2).reshape(128, 5 * 512)),
        identb_h=cst["ident_bf"],
        identf_h=cst["ident_f32"],
        onesr_h=np.ones((1, 512), dtype=np.float32),
        i6_h=cst["i6_f32"],
        onesb_h=cst["ones_bf"],
        onef_h=cst["one_f32"],
        eps_h=cst["eps_col"],
        epat_h=cst["e_pat"],
        sel_h=np.ascontiguousarray(
            cst["sel_pat"].transpose(1, 0, 2).reshape(8, 4 * 128)),
        magic_h=np.random.default_rng().integers(
            -2**31, 2**31, size=(1, 128), dtype=np.int64).astype(np.int32),
    )

    in_maps = []
    for c in range(8):
        j, k = c >> 1, c & 1
        heads = [2 * i + k for i in range(NSLOT)]
        forder = []
        for k2 in range(2):
            for cp in range(4):
                for half in range(2):
                    g = 4 * cp + 2 * half + k2
                    forder.extend(range(g * HD, (g + 1) * HD))
        forder = np.array(forder)

        def slot_cols(off):
            cols = []
            for s in heads:
                cols.extend(range(off + s * HD, off + (s + 1) * HD))
            return np.array(cols)

        m = dict(common)
        m["x_seq"] = np.ascontiguousarray(x[j])
        m["x_res"] = np.ascontiguousarray(x[j, 2048 * k:2048 * (k + 1)])
        m["wq_h"] = np.ascontiguousarray(w_attn[:, slot_cols(0)])
        m["wk_h"] = np.ascontiguousarray(w_attn[:, slot_cols(D)])
        m["wv_h"] = np.ascontiguousarray(w_attn[:, slot_cols(2 * D)])
        m["wp_h"] = np.ascontiguousarray(w_proj[forder, :])
        m["bq_h"] = np.ascontiguousarray(b_attn[slot_cols(0)].reshape(4, 128).T)
        m["bv_h"] = np.ascontiguousarray(b_attn[slot_cols(2 * D)].reshape(1, 512))
        m["bproj_h"] = np.ascontiguousarray(b_proj.reshape(8, 128).T)
        m["bias_tab_h"] = build_bias_tab(k).reshape(128, NSLOT * 64)
        in_maps.append(m)
    return in_maps


_NC_CACHE = {}


def kernel(**inputs):
    if "main" not in _NC_CACHE:
        _NC_CACHE["main"] = build_program()
    nc = _NC_CACHE["main"]
    in_maps = _prep_inputs(inputs)
    res = run_bass_kernel_spmd(nc, in_maps, core_ids=list(range(8)))
    out = np.empty((4, T, D), dtype=np.float32)
    for c in range(8):
        j, k = c >> 1, c & 1
        out[j, 2048 * k:2048 * (k + 1)] = res.results[c]["out_half"]
    return out


# version 05
# speedup vs baseline: 1.2368x; 1.2368x over previous
import numpy as np
import ml_dtypes
from contextlib import ExitStack

import concourse.bass as bass
import concourse.mybir as mybir
from concourse.bass_utils import run_bass_kernel_spmd

F32 = mybir.dt.float32
BF16 = mybir.dt.bfloat16
I32 = mybir.dt.int32
AF = mybir.ActivationFunctionType
ALU = mybir.AluOpType
AX = mybir.AxisListType

T = 4096
D = 1024
NH = 16
HD = 64
NSLOT = 8
NPAIR = 4
E = 6
BEXP = 12.0
EPS = 1e-5

W_SLOT = [128, 256, 512, 896, 1792, 3584, 4096, 4096]
DT_SLOT = [64, 128, 256, 512, 512, 512, 512, 512]
SLOPES = [2.0 ** (-(h + 1) / 2) for h in range(NH)]
MASK_RS = [64, 0, -128, -256, -384]


class Sem:
    def __init__(self, handle):
        self.h = handle
        self.n = 0


class Emitter:
    def __init__(self):
        self.events = []

    def emit(self, engine, fn, waits=()):
        self.events.append(
            (engine, [(s.h, v) for (s, v) in waits if v > 0], fn))

    def replay(self, name, eng):
        last = {}
        for e, waits, fn in self.events:
            if e != name:
                continue
            for h, v in waits:
                if last.get(h, 0) < v:
                    eng.wait_ge(h, v)
                    last[h] = v
            fn(eng)


def chunks_for(slot, t0, dt):
    w = W_SLOT[slot]
    c_lo = max(0, (t0 - w) // 128)
    c_hi = (t0 + dt - 1) // 128
    out = []
    for c in range(c_lo, c_hi + 1):
        masked = (128 * c + 127) > t0
        m = (t0 + dt - 64 - 128 * c) // 64
        out.append((c, masked, m))
    return out


def build_host_consts():
    cst = {}
    cst["ident_bf"] = np.eye(128, dtype=ml_dtypes.bfloat16)
    cst["ident_f32"] = np.eye(128, dtype=np.float32)
    cst["i6_f32"] = np.eye(6, dtype=np.float32)
    cst["ones_bf"] = np.ones((128, 2), dtype=ml_dtypes.bfloat16)
    cst["one_f32"] = np.ones((1, 1), dtype=np.float32)
    cst["eps_col"] = np.full((128, 1), EPS, dtype=np.float32)
    mk = np.zeros((5, 128, 512), dtype=ml_dtypes.bfloat16)
    i = np.arange(128)[:, None]
    j = np.arange(512)[None, :]
    for a, r in enumerate(MASK_RS):
        mk[a] = (i <= j + r).astype(ml_dtypes.bfloat16)
    cst["mask5"] = mk
    ep = np.zeros((7, 128), dtype=np.float32)
    ep[6, 16:32] = 1.0
    for e in range(E):
        ep[e, 32 + 16 * e:32 + 16 * (e + 1)] = 1.0
    cst["e_pat"] = ep
    sel = np.zeros((4, 8, 128), dtype=np.float32)
    for cp in range(4):
        sel[cp, 2 * cp, 0:64] = 1.0
        sel[cp, 2 * cp + 1, 64:128] = 1.0
    cst["sel_pat"] = sel
    return cst


_HOST_CONSTS = build_host_consts()
_MASK_IDX = {r: a for a, r in enumerate(MASK_RS)}


def build_bias_tab(parity):
    tab = np.zeros((128, NSLOT, 64), dtype=np.float32)
    i = np.arange(128, dtype=np.float64)
    for slot in range(NSLOT):
        sl = SLOPES[2 * slot + parity]
        for m in range(64):
            tab[:, slot, m] = (sl * (i - 63 - 64 * m) - BEXP).astype(np.float32)
    return tab


def build_program(taps=(), stop_after=None):
    nc = bass.Bass("TRN2", target_bir_lowering=False,
                   detect_race_conditions=False)
    ap = lambda t: t.ap()
    x_seq = ap(nc.dram_tensor("x_seq", [T, D], F32, kind="ExternalInput"))
    x_res = ap(nc.dram_tensor("x_res", [2048, D], F32, kind="ExternalInput"))
    wq_h = ap(nc.dram_tensor("wq_h", [D, 512], F32, kind="ExternalInput"))
    wk_h = ap(nc.dram_tensor("wk_h", [D, 512], F32, kind="ExternalInput"))
    wv_h = ap(nc.dram_tensor("wv_h", [D, 512], F32, kind="ExternalInput"))
    wp_h = ap(nc.dram_tensor("wp_h", [D, D], F32, kind="ExternalInput"))
    wb_h = ap(nc.dram_tensor("wb_h", [D, D], F32, kind="ExternalInput"))
    acat_h = ap(nc.dram_tensor("acat_h", [D, 128], F32, kind="ExternalInput"))
    bcat_h = ap(nc.dram_tensor("bcat_h", [128, D], F32, kind="ExternalInput"))
    bq_h = ap(nc.dram_tensor("bq_h", [128, 4], F32, kind="ExternalInput"))
    bv_h = ap(nc.dram_tensor("bv_h", [1, 512], F32, kind="ExternalInput"))
    bproj_h = ap(nc.dram_tensor("bproj_h", [128, 8], F32, kind="ExternalInput"))
    bbase_h = ap(nc.dram_tensor("bbase_h", [128, 8], F32, kind="ExternalInput"))
    rb_h = ap(nc.dram_tensor("rb_h", [1, 6], F32, kind="ExternalInput"))
    bias_tab_h = ap(nc.dram_tensor("bias_tab_h", [128, NSLOT * 64], F32, kind="ExternalInput"))
    mask5_h = ap(nc.dram_tensor("mask5_h", [128, 5 * 512], BF16, kind="ExternalInput"))
    identb_h = ap(nc.dram_tensor("identb_h", [128, 128], BF16, kind="ExternalInput"))
    identf_h = ap(nc.dram_tensor("identf_h", [128, 128], F32, kind="ExternalInput"))
    i6_h = ap(nc.dram_tensor("i6_h", [6, 6], F32, kind="ExternalInput"))
    onesb_h = ap(nc.dram_tensor("onesb_h", [128, 2], BF16, kind="ExternalInput"))
    onef_h = ap(nc.dram_tensor("onef_h", [1, 1], F32, kind="ExternalInput"))
    eps_h = ap(nc.dram_tensor("eps_h", [128, 1], F32, kind="ExternalInput"))
    epat_h = ap(nc.dram_tensor("epat_h", [7, 128], F32, kind="ExternalInput"))
    onesr_h = ap(nc.dram_tensor("onesr_h", [1, 512], F32, kind="ExternalInput"))
    sel_h = ap(nc.dram_tensor("sel_h", [8, 4 * 128], F32, kind="ExternalInput"))
    magic_h = ap(nc.dram_tensor("magic_h", [1, 128], I32, kind="ExternalInput"))
    out_half = ap(nc.dram_tensor("out_half", [2048, D], F32, kind="ExternalOutput"))

    y_b = nc.dram_tensor("y_b", [128, 16384], BF16)
    z_b = nc.dram_tensor("z_b", [8, T], F32)
    y_sh = [nc.dram_tensor(f"y_sh{p}", [128, 16384], BF16, addr_space="Shared")
            for p in range(2)]
    z_sh = [nc.dram_tensor(f"z_sh{p}", [8, T], F32, addr_space="Shared")
            for p in range(2)]
    fl_sh = [nc.dram_tensor(f"fl_sh{p}", [1, 128], I32, addr_space="Shared")
             for p in range(2)]

    tap_shapes = {
        "xn1": [128, 8 * 512], "q0": [128, T], "k0": [128, T], "v0": [128, 65],
        "qall": [128, 4 * T], "kall": [128, 4 * T], "vhat": [128, 8 * 32 * 65],
        "xnt0": [128, 1024], "xnf0": [128, 8 * 512],
        "xt0a": [128, 1024], "xt0b": [128, 1024], "cols0": [128, 8],
        "ys0": [64, 64], "zs0": [1, 64],
        "ysend": [128, 16384], "zsend": [8, T], "yrecv": [128, 16384],
        "zrecv": [8, 4096], "ynorm": [128, 8 * 512], "attnp": [128, 8 * 512],
        "x2": [128, 4 * D], "hfm": [128, 512], "wtm": [128, 6],
        "hw": [128, 512], "ffn": [128, 8 * 512],
    }
    tap_outs = {t: ap(nc.dram_tensor("tap_" + t, tap_shapes[t], F32, kind="ExternalOutput"))
                for t in taps}

    ctx0 = ExitStack()
    sb0 = lambda n, s, d=F32: ctx0.enter_context(nc.sbuf_tensor(n, s, d))
    ident_bf = sb0("ident_bf", [128, 128], BF16)
    ident_f = sb0("ident_f", [128, 128], F32)
    i6_sb = sb0("i6_sb", [6, 6], F32)
    ones_bf = sb0("ones_bf", [128, 2], BF16)
    one_f = sb0("one_f", [1, 1], F32)
    eps_col = sb0("eps_col", [128, 1], F32)
    mask5 = sb0("mask5", [128, 5, 512], BF16)
    bias_tab = sb0("bias_tab", [128, NSLOT, 64], F32)
    bq8 = sb0("bq8", [128, 4], F32)
    bq_raw = sb0("bq_raw", [128, 4], F32)
    bv_bc = sb0("bv_bc", [128, 512], F32)
    epat_sb = sb0("epat_sb", [7, 128], F32)
    sel_sb = sb0("sel_sb", [8, 4, 128], F32)
    rb_bc = sb0("rb_bc", [128, 6], F32)
    bproj_c = sb0("bproj_c", [128, 8], F32)
    bbase_c = sb0("bbase_c", [128, 8], F32)
    mg_sb = sb0("mg_sb", [1, 128], I32)

    sctx = ExitStack()
    mks = lambda n: Sem(sctx.enter_context(nc.semaphore(n)))
    d_w, d_c, d_g, d_pull, cc = (mks(n) for n in
                                 ("d_w", "d_c", "d_g", "d_pull", "cc"))
    d_xp = [mks("d_x0"), mks("d_x1")]
    d_yp = [mks("d_y0"), mks("d_y1")]
    d_zp = [mks("d_z0"), mks("d_z1")]
    d_op = [mks("d_o0"), mks("d_o1")]
    v_ln, s_rs, v_nm, s_no, t_tr, v_te = (mks(n) for n in
        ("v_ln", "s_rs", "v_nm", "s_no", "t_tr", "v_te"))
    t_g, s_qe, v_ve = mks("t_g"), mks("s_qe"), mks("v_ve")
    t_qk, s_ex, v_mk, t_av, v_yz = (mks(n) for n in
        ("t_qk", "s_ex", "v_mk", "t_av", "v_yz"))
    bar, v_b, s_b, t_b = mks("bar"), mks("v_b"), mks("s_b"), mks("t_b")
    em = Emitter()

    def replay_block(emx):
        with nc.Block() as blk:
            @blk.sync
            def _(sync):
                emx.replay("sync", sync)

            @blk.gpsimd
            def _(g):
                emx.replay("gpsimd", g)

            @blk.vector
            def _(v):
                emx.replay("vector", v)

            @blk.scalar
            def _(s2):
                emx.replay("scalar", s2)

            @blk.tensor
            def _(t):
                emx.replay("tensor", t)

    def drain_all(emx):
        def fs(sync):
            sync.nop()
        emx.emit("sync", fs, waits=[(d_xp[0], d_xp[0].n), (d_xp[1], d_xp[1].n),
                                    (d_c, d_c.n)])
        def fg(g):
            g.nop()
        emx.emit("gpsimd", fg, waits=[(d_w, d_w.n), (d_g, d_g.n),
                                      (d_yp[0], d_yp[0].n), (d_yp[1], d_yp[1].n),
                                      (d_zp[0], d_zp[0].n), (d_zp[1], d_zp[1].n),
                                      (d_pull, d_pull.n), (cc, cc.n),
                                      (d_op[0], d_op[0].n), (d_op[1], d_op[1].n)])

    ctxA = ExitStack()
    sA = lambda n, s, d=F32: ctxA.enter_context(nc.sbuf_tensor(n, s, d))
    q_pair = [sA(f"q_pair{p}", [128, T], BF16) for p in range(NPAIR)]
    k_pair = [sA(f"k_pair{p}", [128, T], BF16) for p in range(NPAIR)]
    vhat = sA("vhat", [128, NSLOT, 32, 65], BF16)
    z_stage = [sA(f"z_stage{i}", [65, 512], F32) for i in range(2)]
    y_stage = [sA(f"y_stage{i}", [64, 512], BF16) for i in range(2)]
    pbuf = [sA(f"pbuf{i}", [128, 512], BF16) for i in range(2)]
    pmt = [sA(f"pmt{i}", [128, 512], BF16) for i in range(2)]

    ctxA0 = ExitStack()
    sA0 = lambda n, s, d=F32: ctxA0.enter_context(nc.sbuf_tensor(n, s, d))
    x_t = [sA0(f"x_t{i}", [128, D], F32) for i in range(2)]
    sq_bf = sA0("sq_bf", [128, D], BF16)
    xn1_tm = [sA0(f"xn1_tm{i}", [128, D], BF16) for i in range(2)]
    xn1_fm = [sA0(f"xn1_fm{i}", [128, 8, 512], BF16) for i in range(2)]
    wq_bf = sA0("wq_bf", [128, 8, 512], BF16)
    wk_bf = sA0("wk_bf", [128, 8, 512], BF16)
    wv_bf = sA0("wv_bf", [128, 8, 512], BF16)
    scol = [sA0(f"scol{i}", [128, 1], F32) for i in range(2)]
    qcol = [sA0(f"qcol{i}", [128, 1], F32) for i in range(2)]
    mcol = [sA0(f"mcol{i}", [128, 1], F32) for i in range(2)]
    m2col = sA0("m2col", [128, 1], F32)
    vcol = [sA0(f"vcol{i}", [128, 1], F32) for i in range(2)]
    lncol = sA0("lncol", [128, 1], F32)
    rscol = [sA0(f"rscol{i}", [128, 1], F32) for i in range(2)]
    nmcol = [sA0(f"nmcol{i}", [128, 1], F32) for i in range(2)]

    pctxA = ExitStack()
    tr4 = [pctxA.enter_context(nc.psum_tensor(f"tr4_{i}", [128, 128], F32))
           for i in range(4)]
    gen_ps = [pctxA.enter_context(nc.psum_tensor(f"gen_ps{i}", [128, 512], F32)) for i in range(2)]
    ps_y = [pctxA.enter_context(nc.psum_tensor(f"ps_y{i}", [128, 512], F32)) for i in range(2)]

    def load_consts(sync):
        sync.dma_start(out=ident_bf[:], in_=identb_h).then_inc(d_c.h, 16)
        sync.dma_start(out=ident_f[:], in_=identf_h).then_inc(d_c.h, 16)
        sync.dma_start(out=i6_sb[:], in_=i6_h).then_inc(d_c.h, 16)
        sync.dma_start(out=ones_bf[:], in_=onesb_h).then_inc(d_c.h, 16)
        sync.dma_start(out=one_f[:], in_=onef_h).then_inc(d_c.h, 16)
        sync.dma_start(out=eps_col[:], in_=eps_h).then_inc(d_c.h, 16)
        sync.dma_start(out=mask5[:].rearrange("p a c -> p (a c)"), in_=mask5_h).then_inc(d_c.h, 16)
        sync.dma_start(out=bias_tab[:].rearrange("p a c -> p (a c)"), in_=bias_tab_h).then_inc(d_c.h, 16)
        sync.dma_start(out=bq_raw[:], in_=bq_h).then_inc(d_c.h, 16)
        sync.dma_start(out=bv_bc[:], in_=bv_h.broadcast_to((128, 512))).then_inc(d_c.h, 16)
        sync.dma_start(out=epat_sb[:], in_=epat_h).then_inc(d_c.h, 16)
        sync.dma_start(out=sel_sb[:].rearrange("p a c -> p (a c)"), in_=sel_h).then_inc(d_c.h, 16)
        sync.dma_start(out=rb_bc[:], in_=rb_h.broadcast_to((128, 6))).then_inc(d_c.h, 16)
        sync.dma_start(out=bproj_c[:], in_=bproj_h).then_inc(d_c.h, 16)
        sync.dma_start(out=bbase_c[:], in_=bbase_h).then_inc(d_c.h, 16)
        sync.dma_start(out=mg_sb[:], in_=magic_h).then_inc(d_c.h, 16)
    em.emit("sync", load_consts)
    d_c.n += 16 * 16
    NCONST = d_c.n

    def load_wattn(g):
        for kc in range(8):
            g.dma_start(out=wq_bf[:, kc, :], in_=wq_h[128 * kc:128 * (kc + 1), :]).then_inc(d_w.h, 16)
            g.dma_start(out=wk_bf[:, kc, :], in_=wk_h[128 * kc:128 * (kc + 1), :]).then_inc(d_w.h, 16)
            g.dma_start(out=wv_bf[:, kc, :], in_=wv_h[128 * kc:128 * (kc + 1), :]).then_inc(d_w.h, 16)
    em.emit("gpsimd", load_wattn)
    d_w.n += 24 * 16
    NW_A = d_w.n

    def prep_v(v):
        v.tensor_scalar_mul(bq8[:], bq_raw[:], 0.125)
        v.memset(vhat[:, :, :, 64:65], 1.0)
    em.emit("vector", prep_v, waits=[(d_c, NCONST)])

    if stop_after == "consts":
        drain_all(em)
        replay_block(em)
        return nc

    norm_done = {}
    rstd_done = {}
    gen_done = {}
    trA_free = [(v_te, 0)] * 4
    trA_i = [0]
    ps0_free = (s_qe, 0)
    ps1_free = (s_qe, 0)
    n_slabs_a0 = 1 if stop_after in ("a0s1", "a0tile", "a0v", "a0q") else 8
    for slab in range(n_slabs_a0):
        t0s = slab * 512
        for sub in range(4):
            tau = slab * 4 + sub
            bi = tau % 2

            xload_w = [(s_no, norm_done.get(tau - 2, 0))]
            if tau == 2 and "xt0b" in tap_outs:
                xload_w.append((d_g, xt0b_guard))

            def xload(sync, tau=tau, bi=bi):
                sync.dma_start(out=x_t[bi][:], in_=x_seq[128 * tau:128 * (tau + 1), :]
                               ).then_inc(d_xp[bi].h, 16)
            em.emit("sync", xload, waits=xload_w)
            d_xp[bi].n += 16
            if tau == 0 and "xt0a" in tap_outs:
                def tapxt0a(g):
                    g.dma_start(out=tap_outs["xt0a"], in_=x_t[0][:]).then_inc(d_g.h, 16)
                em.emit("gpsimd", tapxt0a, waits=[(d_xp[0], d_xp[0].n)])
                d_g.n += 16
                xt0a_guard = d_g.n

            def vstat(v, bi=bi):
                v.tensor_reduce(scol[bi][:], x_t[bi][:], AX.X, ALU.add)
                v.scalar_tensor_tensor(sq_bf[:], x_t[bi][:], 1.0, x_t[bi][:],
                                       ALU.mult, ALU.mult)
                v.tensor_reduce(qcol[bi][:], sq_bf[:], AX.X, ALU.add)
                v.tensor_scalar_mul(mcol[bi][:], scol[bi][:], 1.0 / D)
                v.drain()
                v.tensor_mul(m2col[:], mcol[bi][:], mcol[bi][:])
                v.drain()
                v.scalar_tensor_tensor(vcol[bi][:], qcol[bi][:], 1.0 / D, m2col[:],
                                       ALU.mult, ALU.subtract)
                v.drain().then_inc(v_ln.h, 1)
            vstat_w = [(d_xp[bi], d_xp[bi].n), (s_rs, rstd_done.get(tau - 2, 0))]
            if tau == 0 and "xt0a" in tap_outs:
                vstat_w.append((d_g, xt0a_guard))
            if tau == 2 and "cols0" in tap_outs:
                vstat_w.append((d_g, cols0_guard))
            em.emit("vector", vstat, waits=vstat_w)
            v_ln.n += 1

            def srstd(s, bi=bi):
                s.activation(lncol[:], vcol[bi][:], AF.Ln, bias=eps_col[:, :])
                s.drain()
                s.activation(rscol[bi][:], lncol[:], AF.Exp, scale=-0.5)
                s.drain().then_inc(s_rs.h, 1)
            em.emit("scalar", srstd, waits=[(v_ln, v_ln.n)])
            s_rs.n += 1
            rstd_done[tau] = s_rs.n

            def vnmr(v, bi=bi):
                v.scalar_tensor_tensor(nmcol[bi][:], mcol[bi][:], -1.0, rscol[bi][:],
                                       ALU.mult, ALU.mult)
                v.drain().then_inc(v_nm.h, 1)
            em.emit("vector", vnmr, waits=[(s_rs, s_rs.n),
                                           (s_no, norm_done.get(tau - 2, 0))])
            v_nm.n += 1
            if tau == 0 and "cols0" in tap_outs:
                def tapcols0(g):
                    for ci, tt in enumerate((scol[0], qcol[0], mcol[0], m2col,
                                             vcol[0], lncol, rscol[0], nmcol[0])):
                        g.dma_start(out=tap_outs["cols0"][:, ci:ci + 1],
                                    in_=tt[:],
                                    allow_slow_non_contiguous=True).then_inc(d_g.h, 16)
                em.emit("gpsimd", tapcols0, waits=[(v_nm, v_nm.n)])
                d_g.n += 8 * 16
                cols0_guard = d_g.n

            snorm_w = [(v_nm, v_nm.n)]
            if tau == 2 and "xnt0" in tap_outs:
                snorm_w.append((d_g, xnt0_guard))

            def snorm(s, bi=bi):
                s.activation(xn1_tm[bi][:], x_t[bi][:], AF.Identity,
                             bias=nmcol[bi][:, :], scale=rscol[bi][:, :])
                s.drain().then_inc(s_no.h, 1)
            em.emit("scalar", snorm, waits=snorm_w)
            s_no.n += 1
            norm_done[tau] = s_no.n
            if tau == 0 and "xnt0" in tap_outs:
                def tapxnt0(g):
                    g.dma_start(out=tap_outs["xnt0"], in_=xn1_tm[0][:]).then_inc(d_g.h, 16)
                em.emit("gpsimd", tapxnt0, waits=[(s_no, s_no.n)])
                d_g.n += 16
                xnt0_guard = d_g.n
            if tau == 0 and "xt0b" in tap_outs:
                def tapxt0b(g):
                    g.dma_start(out=tap_outs["xt0b"], in_=x_t[0][:]).then_inc(d_g.h, 16)
                em.emit("gpsimd", tapxt0b, waits=[(s_no, s_no.n)])
                d_g.n += 16
                xt0b_guard = d_g.n

            for c in range(8):
                bk = trA_i[0] % 4
                trA_i[0] += 1

                def ttr(t, bi=bi, c=c, bk=bk):
                    t.matmul(tr4[bk][:], xn1_tm[bi][:, 128 * c:128 * (c + 1)],
                             ident_bf[:], start=True, stop=True).then_inc(t_tr.h, 1)
                ttr_w = [(s_no, s_no.n), (d_c, NCONST), trA_free[bk],
                         (t_g, gen_done.get(slab - 2, 0))]
                if slab == 2 and sub == 0 and c == 0 and "xnf0" in tap_outs:
                    ttr_w.append((d_g, xnf0_guard))
                em.emit("tensor", ttr, waits=ttr_w)
                t_tr.n += 1
                ttn = t_tr.n

                def vte(v, c=c, sub=sub, slab=slab, bk=bk, ttn=ttn,
                        last=(sub == 3 and c == 7)):
                    ins = v.tensor_copy(xn1_fm[slab % 2][:, c, 128 * sub:128 * (sub + 1)],
                                        tr4[bk][:])
                    if last:
                        v.drain().then_inc(v_te.h, 1)
                    else:
                        ins.then_inc(v_te.h, 1)
                em.emit("vector", vte, waits=[(t_tr, ttn)])
                v_te.n += 1
                trA_free[bk] = (v_te, v_te.n)

        if slab == 0 and "xnf0" in tap_outs:
            def tapxnf0(g):
                g.dma_start(out=tap_outs["xnf0"],
                            in_=xn1_fm[0][:].rearrange("p c t -> p (c t)")).then_inc(d_g.h, 16)
            em.emit("gpsimd", tapxnf0, waits=[(v_te, v_te.n)])
            d_g.n += 16
            xnf0_guard = d_g.n

        xn = xn1_fm[slab % 2]
        if stop_after in ("a0tile", "a0tile8"):
            gen_done[slab] = t_g.n
            continue
        for p in range(NPAIR if stop_after != "a0v" else 0):
            def qmm(t, p=p, xn=xn):
                for kc in range(8):
                    ins = t.matmul(gen_ps[0][:], wq_bf[:, kc, 128 * p:128 * (p + 1)],
                             xn[:, kc, :], start=(kc == 0), stop=(kc == 7))
                ins.then_inc(t_g.h, 1)
            em.emit("tensor", qmm, waits=[(v_te, v_te.n), (d_w, NW_A), ps0_free])
            t_g.n += 1
            tgn = t_g.n

            def qev(s, p=p, t0s=t0s):
                s.activation(q_pair[p][:, t0s:t0s + 512], gen_ps[0][:], AF.Identity,
                             scale=0.125, bias=bq8[:, p:p + 1])
                s.drain().then_inc(s_qe.h, 1)
            em.emit("scalar", qev, waits=[(t_g, tgn)])
            s_qe.n += 1
            ps0_free = (s_qe, s_qe.n)

            def kmm(t, p=p, xn=xn):
                for kc in range(8):
                    ins = t.matmul(gen_ps[1][:], wk_bf[:, kc, 128 * p:128 * (p + 1)],
                             xn[:, kc, :], start=(kc == 0), stop=(kc == 7))
                ins.then_inc(t_g.h, 1)
            em.emit("tensor", kmm, waits=[ps1_free])
            t_g.n += 1
            tgn = t_g.n

            def kev(s, p=p, t0s=t0s):
                s.copy(out=k_pair[p][:, t0s:t0s + 512], in_=gen_ps[1][:])
                s.drain().then_inc(s_qe.h, 1)
            em.emit("scalar", kev, waits=[(t_g, tgn)])
            s_qe.n += 1
            ps1_free = (s_qe, s_qe.n)

        for sub in range(4 if stop_after != "a0q" else 0):
            tau = slab * 4 + sub

            def vmm(t, sub=sub, xn=xn):
                for kc in range(8):
                    ins = t.matmul(gen_ps[0][:], xn[:, kc, 128 * sub:128 * (sub + 1)],
                             wv_bf[:, kc, :], start=(kc == 0), stop=(kc == 7))
                ins.then_inc(t_g.h, 1)
            em.emit("tensor", vmm, waits=[ps0_free])
            t_g.n += 1
            tgn = t_g.n

            def vev(v, tau=tau):
                v.tensor_tensor(vhat[:, :, tau, 0:64],
                                gen_ps[0][:].rearrange("p (s d) -> p s d", s=8, d=64),
                                bv_bc[:].rearrange("p (s d) -> p s d", s=8, d=64),
                                ALU.add)
                v.drain().then_inc(v_ve.h, 1)
            em.emit("vector", vev, waits=[(t_g, tgn), (d_c, NCONST)])
            v_ve.n += 1
            ps0_free = (v_ve, v_ve.n)
        gen_done[slab] = t_g.n
    A0_SQE, A0_VVE = s_qe.n, v_ve.n

    if "xn1" in tap_outs:
        def tapxn(g):
            g.dma_start(out=tap_outs["xn1"],
                        in_=xn1_fm[1][:].rearrange("p c t -> p (c t)")).then_inc(d_g.h, 16)
        em.emit("gpsimd", tapxn, waits=[(v_te, v_te.n)])
        d_g.n += 16
    if "q0" in tap_outs:
        def tapq(g):
            g.dma_start(out=tap_outs["q0"], in_=q_pair[0][:]).then_inc(d_g.h, 16)
            g.dma_start(out=tap_outs["k0"], in_=k_pair[0][:]).then_inc(d_g.h, 16)
            g.dma_start(out=tap_outs["v0"], in_=vhat[:, 0, 3, :]).then_inc(d_g.h, 16)
        em.emit("gpsimd", tapq, waits=[(s_qe, s_qe.n), (v_ve, v_ve.n)])
        d_g.n += 48
    if "qall" in tap_outs:
        def tapqa(g):
            for p in range(NPAIR):
                g.dma_start(out=tap_outs["qall"][:, p * T:(p + 1) * T],
                            in_=q_pair[p][:]).then_inc(d_g.h, 16)
                g.dma_start(out=tap_outs["kall"][:, p * T:(p + 1) * T],
                            in_=k_pair[p][:]).then_inc(d_g.h, 16)
            g.dma_start(out=tap_outs["vhat"],
                        in_=vhat[:].rearrange("p a b c -> p (a b c)")).then_inc(d_g.h, 16)
        em.emit("gpsimd", tapqa, waits=[(s_qe, s_qe.n), (v_ve, v_ve.n)])
        d_g.n += 9 * 16

    if stop_after in ("a0", "a0s1", "a0tile", "a0tile8", "a0v", "a0q"):
        drain_all(em)
        replay_block(em)
        return nc

    ps_s = gen_ps
    gi = 0
    exp_of = {}
    msk_of = {}
    av_of = {}
    slab_evac = {}
    slab_idx = 0
    for slot in range(NSLOT):
        pr, half = slot >> 1, slot & 1
        base = 64 * half
        cp = slot >> 1
        dt = DT_SLOT[slot]
        for b in range(T // dt):
            t0 = b * dt
            cl = chunks_for(slot, t0, dt)
            for ci, (c, masked, m) in enumerate(cl):
                pb = gi % 2
                qk_waits = [(s_ex, exp_of.get(gi - 2, 0))]
                if gi < 2:
                    qk_waits += [(s_qe, A0_SQE), (v_ve, A0_VVE)]

                def qk(t, pb=pb, base=base, pr=pr, c=c, t0=t0, dt=dt):
                    t.matmul(ps_s[pb][:, 0:dt],
                             k_pair[pr][base:base + 64, 128 * c:128 * (c + 1)],
                             q_pair[pr][base:base + 64, t0:t0 + dt],
                             start=True, stop=True).then_inc(t_qk.h, 1)
                em.emit("tensor", qk, waits=qk_waits)
                t_qk.n += 1

                def ex(s, pb=pb, masked=masked, slot=slot, m=m, dt=dt):
                    dst = pmt[pb] if masked else pbuf[pb]
                    s.activation(dst[:, 0:dt], ps_s[pb][:, 0:dt], AF.Exp,
                                 bias=bias_tab[:, slot, m:m + 1])
                    s.drain().then_inc(s_ex.h, 1)
                em.emit("scalar", ex, waits=[(t_qk, t_qk.n), (d_c, NCONST),
                                             (t_av, av_of.get(gi - 2, 0)),
                                             (v_mk, msk_of.get(gi - 2, 0))])
                s_ex.n += 1
                exp_of[gi] = s_ex.n

                if masked:
                    mi = _MASK_IDX[t0 - 128 * c]

                    def mkf(v, pb=pb, mi=mi, dt=dt):
                        v.tensor_tensor(pbuf[pb][:, 0:dt], pmt[pb][:, 0:dt],
                                        mask5[:, mi, 0:dt], ALU.mult)
                        v.drain().then_inc(v_mk.h, 1)
                    em.emit("vector", mkf, waits=[(s_ex, s_ex.n), (d_c, NCONST),
                                                  (t_av, av_of.get(gi - 2, 0))])
                    v_mk.n += 1
                    msk_of[gi] = v_mk.n
                    av_wait = [(v_mk, v_mk.n)]
                else:
                    av_wait = [(s_ex, s_ex.n)]

                if ci == 0:
                    av_wait.append((v_yz, slab_evac.get(slab_idx - 2, 0)))

                def av(t, pb=pb, slot=slot, c=c, dt=dt, first=(ci == 0),
                       last=(ci == len(cl) - 1), sy=slab_idx % 2):
                    t.matmul(ps_y[sy][0:65, 0:dt], vhat[:, slot, c, :],
                             pbuf[pb][:, 0:dt], start=first, stop=last
                             ).then_inc(t_av.h, 1)
                em.emit("tensor", av, waits=av_wait)
                t_av.n += 1
                av_of[gi] = t_av.n
                gi += 1

            sy = slab_idx % 2
            ccol0 = cp * 4096 + t0
            ob = slab_idx % 2

            yev_w = [(t_av, t_av.n), (d_yp[ob], d_yp[ob].n)]
            if slab_idx == 2 and "ys0" in tap_outs:
                yev_w.append((d_g, ys0_guard))

            def yev(v, sy=sy, ob=ob, dt=dt):
                v.tensor_copy(y_stage[ob][:, 0:dt],
                              ps_y[sy][0:64, 0:dt])
                v.drain().then_inc(v_yz.h, 1)
            em.emit("vector", yev, waits=yev_w)
            v_yz.n += 1
            yn = v_yz.n

            def ydma(g, ob=ob, ccol0=ccol0, dt=dt, half=half):
                g.dma_start(out=y_b.ap()[64 * half:64 * half + 64, ccol0:ccol0 + dt],
                            in_=y_stage[ob][:, 0:dt]).then_inc(d_yp[ob].h, 16)
            em.emit("gpsimd", ydma, waits=[(v_yz, yn)])
            d_yp[ob].n += 16

            zb = slab_idx % 2

            zev_w = [(t_av, t_av.n), (d_zp[zb], d_zp[zb].n)]
            if slab_idx == 2 and "ys0" in tap_outs:
                zev_w.append((d_g, ys0_guard))

            def zev(v, sy=sy, zb=zb, dt=dt):
                v.tensor_copy(z_stage[zb][64:65, 0:dt],
                              ps_y[sy][64:65, 0:dt])
                v.drain().then_inc(v_yz.h, 1)
            em.emit("vector", zev, waits=zev_w)
            v_yz.n += 1
            zn = v_yz.n

            def zdma(g, zb=zb, slot=slot, t0=t0, dt=dt):
                g.dma_start(out=z_b.ap()[slot:slot + 1, t0:t0 + dt],
                            in_=z_stage[zb][64:65, 0:dt]).then_inc(d_zp[zb].h, 16)
            em.emit("gpsimd", zdma, waits=[(v_yz, zn)])
            d_zp[zb].n += 16
            slab_evac[slab_idx] = v_yz.n
            if slab_idx == 0 and "ys0" in tap_outs:
                def tapys0(g):
                    g.dma_start(out=tap_outs["ys0"], in_=y_stage[0][:, 0:64]).then_inc(d_g.h, 16)
                    g.dma_start(out=tap_outs["zs0"], in_=z_stage[0][64:65, 0:64]).then_inc(d_g.h, 16)
                em.emit("gpsimd", tapys0, waits=[(v_yz, v_yz.n)])
                d_g.n += 32
                ys0_guard = d_g.n
            slab_idx += 1
    A1_VYZ = v_yz.n

    if "ysend" in tap_outs:
        def tapy(g):
            g.dma_start(out=tap_outs["ysend"], in_=y_b.ap()[:, :]).then_inc(d_g.h, 16)
            g.dma_start(out=tap_outs["zsend"], in_=z_b.ap()[:, :]).then_inc(d_g.h, 16)
        em.emit("gpsimd", tapy,
                waits=[(v_yz, A1_VYZ), (d_yp[0], d_yp[0].n), (d_yp[1], d_yp[1].n),
                       (d_zp[0], d_zp[0].n), (d_zp[1], d_zp[1].n)])
        d_g.n += 32

    if stop_after == "a1":
        drain_all(em)
        replay_block(em)
        return nc

    STAGE_WAITS = [(d_yp[0], d_yp[0].n), (d_yp[1], d_yp[1].n),
                   (d_zp[0], d_zp[0].n), (d_zp[1], d_zp[1].n)]
    CC0 = cc.n

    def share(g):
        with g.register("r_par_s") as r_par:
            g.reg_mod(r_par, g.partition_id(), 2)
            with g.If_eq(r_par, 0):
                g.dma_start(out=y_sh[0].ap()[:, :], in_=y_b.ap()[:, :]).then_inc(cc.h, 16)
                g.dma_start(out=z_sh[0].ap()[:, :], in_=z_b.ap()[:, :]).then_inc(cc.h, 16)
                g.wait_ge(cc.h, CC0 + 32)
                g.dma_start(out=fl_sh[0].ap()[:, :], in_=mg_sb[:]).then_inc(cc.h, 16)
            with g.Else():
                g.dma_start(out=y_sh[1].ap()[:, :], in_=y_b.ap()[:, :]).then_inc(cc.h, 16)
                g.dma_start(out=z_sh[1].ap()[:, :], in_=z_b.ap()[:, :]).then_inc(cc.h, 16)
                g.wait_ge(cc.h, CC0 + 32)
                g.dma_start(out=fl_sh[1].ap()[:, :], in_=mg_sb[:]).then_inc(cc.h, 16)
    em.emit("gpsimd", share, waits=STAGE_WAITS + [(d_g, d_g.n), (d_c, NCONST)])
    cc.n += 48

    if stop_after == "coll":
        drain_all(em)
        replay_block(em)
        return nc

    def binc(e):
        e.nop()
        e.drain().then_inc(bar.h, 1)
    em.emit("tensor", binc, waits=[(t_av, t_av.n)])
    em.emit("scalar", binc, waits=[(s_ex, s_ex.n)])
    em.emit("vector", binc, waits=[(v_yz, v_yz.n)])
    em.emit("sync", binc, waits=[(d_xp[0], d_xp[0].n), (d_xp[1], d_xp[1].n),
                                 (d_c, NCONST)])
    em.emit("gpsimd", binc, waits=STAGE_WAITS + [(d_g, d_g.n)])
    bar.n = 5

    replay_block(em)
    em = Emitter()
    pctxA.close()
    ctxA0.close()
    ctxA.close()

    ctxB = ExitStack()
    sB = lambda n, s, d=F32: ctxB.enter_context(nc.sbuf_tensor(n, s, d))
    y_recv = sB("y_recv", [128, 16384], BF16)
    y_nslab = sB("y_nslab", [128, 8, 512], BF16)
    z_recv = sB("z_recv", [8, 2 * 2048], F32)
    zinv_sb = sB("zinv_sb", [128, 512], F32)
    wp_bf = sB("wp_bf", [128, 8, D], BF16)
    wb_all = sB("wb_all", [128, 8, D], BF16)
    wbt_bf = sB("wbt_bf", [128, 8, 8, 128], BF16)
    acat_bf = sB("acat_bf", [128, 8, 128], BF16)
    bcat_bf = sB("bcat_bf", [128, D], BF16)
    x_sub = [sB(f"x_sub{i}", [128, D], F32) for i in range(2)]
    x2_tm = sB("x2_tm", [128, 4, D], F32)
    attnp_fm = sB("attnp_fm", [128, 8, 512], BF16)
    ffn_fm = attnp_fm
    xn2_tm = sB("xn2_tm", [128, D], BF16)
    xn2_fm = sB("xn2_fm", [128, 8, 512], BF16)
    asq_bf = sB("asq_bf", [128, 512], BF16)
    sqb_bf = sB("sqb_bf", [128, D], BF16)
    h_fm = sB("h_fm", [128, 512], F32)
    hw_bf = sB("hw_bf", [128, 512], BF16)
    w_fm7 = sB("w_fm7", [7, 512], F32)
    ssum_row = sB("ssum_row", [1, 512], F32)
    ssq_row = sB("ssq_row", [1, 512], F32)
    g0 = sB("g0", [128, 6], F32)
    gmx = sB("gmx", [128, 1], F32)
    gmxn = sB("gmxn", [128, 1], F32)
    ge = sB("ge", [128, 6], F32)
    ges = sB("ges", [128, 1], F32)
    gesr = sB("gesr", [128, 1], F32)
    gate = sB("gate", [128, 6], F32)
    msk1 = sB("msk1", [128, 6], F32)
    gg2 = sB("gg2", [128, 6], F32)
    gmx2 = sB("gmx2", [128, 1], F32)
    gmx3 = sB("gmx3", [128, 1], F32)
    msk2 = sB("msk2", [128, 6], F32)
    wun = sB("wun", [128, 6], F32)
    wsum = sB("wsum", [128, 1], F32)
    wsum2 = sB("wsum2", [128, 1], F32)
    wsr = sB("wsr", [128, 1], F32)
    w_tm = sB("w_tm", [128, 6], F32)
    mrow = sB("mrow", [1, 512], F32)
    m2row = sB("m2row", [1, 512], F32)
    vrow = sB("vrow", [1, 512], F32)
    lnrow = sB("lnrow", [1, 512], F32)
    rsrow = sB("rsrow", [1, 512], F32)
    rn_cols = sB("rn_cols", [128, 4, 2], F32)
    m_cols = sB("m_cols", [128, 4], F32)
    ln_f32 = sB("ln_f32", [128, D], F32)
    out_tm = [sB(f"out_tm{i}", [128, D], F32) for i in range(2)]
    b_scol = sB("b_scol", [128, 1], F32)
    b_qcol = sB("b_qcol", [128, 1], F32)
    b_mcol = sB("b_mcol", [128, 1], F32)
    b_m2 = sB("b_m2", [128, 1], F32)
    b_vcol = sB("b_vcol", [128, 1], F32)
    b_ln = sB("b_ln", [128, 1], F32)
    b_rs = sB("b_rs", [128, 1], F32)
    b_nm = sB("b_nm", [128, 1], F32)

    pctxB = ExitStack()
    ps_fm = [pctxB.enter_context(nc.psum_tensor(f"ps_fm{i}", [128, 512], F32)) for i in range(2)]
    ps_h = pctxB.enter_context(nc.psum_tensor("ps_h", [128, 512], F32))
    ps_misc = pctxB.enter_context(nc.psum_tensor("ps_misc", [128, 512], F32))
    trb = [pctxB.enter_context(nc.psum_tensor(f"trb{i}", [128, 128], F32))
           for i in range(4)]
    trb_free = [(v_b, 0)] * 4
    trb_i = [0]

    def trb_next():
        b = trb_i[0] % 4
        trb_i[0] += 1
        return b

    def loadB(g):
        for kc in range(8):
            g.dma_start(out=wp_bf[:, kc, :], in_=wp_h[128 * kc:128 * (kc + 1), :]).then_inc(d_w.h, 16)
        for kc in range(8):
            g.dma_start(out=acat_bf[:, kc, :], in_=acat_h[128 * kc:128 * (kc + 1), :]).then_inc(d_w.h, 16)
        g.dma_start(out=bcat_bf[:], in_=bcat_h).then_inc(d_w.h, 16)
        for ot in range(8):
            g.dma_start(out=wb_all[:, ot, :], in_=wb_h[128 * ot:128 * (ot + 1), :]).then_inc(d_w.h, 16)
        g.dma_start(out=w_fm7[6:7, :], in_=onesr_h).then_inc(d_w.h, 16)
    em.emit("gpsimd", loadB, waits=[(bar, 5)])
    d_w.n += 26 * 16
    WB_DW = d_w.n

    for ot in range(8):
        for ic in range(8):
            bk = trb_next()

            def wbt_mm(t, ot=ot, ic=ic, bk=bk):
                t.matmul(trb[bk][:], wb_all[:, ot, 128 * ic:128 * (ic + 1)],
                         ident_bf[:], start=True, stop=True).then_inc(t_b.h, 1)
            em.emit("tensor", wbt_mm, waits=[(d_w, WB_DW), (bar, 5), trb_free[bk]])
            t_b.n += 1
            tbn = t_b.n

            def wbt_ev(v, ot=ot, ic=ic, bk=bk, tbn=tbn,
                       last=(ot == 7 and ic == 7)):
                ins = v.tensor_copy(wbt_bf[:, ic, ot, :], trb[bk][:])
                if last:
                    v.drain().then_inc(v_b.h, 1)
                else:
                    ins.then_inc(v_b.h, 1)
            em.emit("vector", wbt_ev, waits=[(t_b, tbn)])
            v_b.n += 1
            trb_free[bk] = (v_b, v_b.n)
    WBT_VB = v_b.n

    if stop_after == "wbt":
        drain_all(em)
        replay_block(em)
        pctxB.close()
        ctxB.close()
        sctx.close()
        ctx0.close()
        return nc

    def pull(g):
        with g.register("rm1") as rm1, g.register("rm2") as rm2, \
             g.register("rv1") as rv1, g.register("rv2") as rv2, \
             g.register("cond") as cond:
            g.reg_load(rm1, mg_sb[0:1, 0:1])
            g.reg_load(rm2, mg_sb[0:1, 1:2])
            for p in range(2):
                g.reg_mov(cond, 1)
                with g.While(cond):
                    g.reg_load(rv1, fl_sh[p].ap()[0:1, 0:1])
                    g.reg_load(rv2, fl_sh[p].ap()[0:1, 1:2])
                    g.reg_alu(rv1, rv1, rm1, ALU.not_equal)
                    g.reg_alu(rv2, rv2, rm2, ALU.not_equal)
                    g.reg_alu(cond, rv1, rv2, ALU.logical_or)
        with g.register("r_par") as r_par:
            g.reg_mod(r_par, g.partition_id(), 2)
            with g.If_eq(r_par, 0):
                for k2 in range(2):
                    g.dma_start(
                        out=y_recv[:, k2 * 8192:(k2 + 1) * 8192].rearrange(
                            "p (c t) -> p c t", c=4, t=2048),
                        in_=y_sh[k2].ap().rearrange("p (c t) -> p c t", c=4, t=4096)[:, :, 0:2048],
                    ).then_inc(d_pull.h, 16)
                    g.dma_start(out=z_recv[:, k2 * 2048:(k2 + 1) * 2048],
                                in_=z_sh[k2].ap()[:, 0:2048]).then_inc(d_pull.h, 16)
            with g.Else():
                for k2 in range(2):
                    g.dma_start(
                        out=y_recv[:, k2 * 8192:(k2 + 1) * 8192].rearrange(
                            "p (c t) -> p c t", c=4, t=2048),
                        in_=y_sh[k2].ap().rearrange("p (c t) -> p c t", c=4, t=4096)[:, :, 2048:4096],
                    ).then_inc(d_pull.h, 16)
                    g.dma_start(out=z_recv[:, k2 * 2048:(k2 + 1) * 2048],
                                in_=z_sh[k2].ap()[:, 2048:4096]).then_inc(d_pull.h, 16)
    em.emit("gpsimd", pull, waits=[(bar, 5)])
    d_pull.n += 64

    if "yrecv" in tap_outs:
        def tapyr(g):
            g.dma_start(out=tap_outs["yrecv"], in_=y_recv[:]).then_inc(d_g.h, 16)
            g.dma_start(out=tap_outs["zrecv"], in_=z_recv[:]).then_inc(d_g.h, 16)
        em.emit("gpsimd", tapyr, waits=[(d_pull, 64)])
        d_g.n += 32

    if stop_after == "pull":
        drain_all(em)
        replay_block(em)
        pctxB.close()
        ctxB.close()
        sctx.close()
        ctx0.close()
        return nc

    def fm_ln_stats(src_fm, ev_sems):
        stat_t = [0]
        for oc in range(8):
            sq_w = [ev_sems[oc]]
            if oc > 0:
                sq_w.append((t_b, stat_t[0]))

            def sqp(v, oc=oc, src_fm=src_fm):
                v.scalar_tensor_tensor(asq_bf[:], src_fm[:, oc, :], 1.0,
                                       src_fm[:, oc, :], ALU.mult, ALU.mult
                                       )
                v.drain().then_inc(v_b.h, 1)
            em.emit("vector", sqp, waits=sq_w)
            v_b.n += 1
            vbn = v_b.n

            def stmm(t, oc=oc, src_fm=src_fm):
                t.matmul(ps_misc[0:1, 0:512], ones_bf[:, 0:1], src_fm[:, oc, :],
                         start=(oc == 0), stop=(oc == 7))
                t.matmul(ps_h[0:1, 0:512], ones_bf[:, 0:1], asq_bf[:],
                         start=(oc == 0), stop=(oc == 7)).then_inc(t_b.h, 1)
            em.emit("tensor", stmm, waits=[(v_b, vbn)])
            t_b.n += 1
            stat_t[0] = t_b.n

    def row_stats_to_cols():
        def revac(v):
            v.tensor_copy(ssum_row[:], ps_misc[0:1, 0:512])
            v.tensor_copy(ssq_row[:], ps_h[0:1, 0:512])
            v.drain().then_inc(v_b.h, 1)
        em.emit("vector", revac, waits=[(t_b, t_b.n)])
        v_b.n += 1

        def rmath_v(v):
            v.tensor_scalar_mul(mrow[:], ssum_row[:], 1.0 / D)
            v.drain()
            v.tensor_mul(m2row[:], mrow[:], mrow[:])
            v.drain()
            v.scalar_tensor_tensor(vrow[:], ssq_row[:], 1.0 / D, m2row[:],
                                   ALU.mult, ALU.subtract)
            v.drain().then_inc(v_b.h, 1)
        em.emit("vector", rmath_v)
        v_b.n += 1

        def rmath_s(s):
            s.activation(lnrow[:], vrow[:], AF.Ln, bias=eps_col[0:1, :])
            s.drain()
            s.activation(rsrow[:], lnrow[:], AF.Exp, scale=-0.5)
            s.drain().then_inc(s_b.h, 1)
        em.emit("scalar", rmath_s, waits=[(v_b, v_b.n)])
        s_b.n += 1
        sbn = s_b.n

        for sub in range(4):
            bkm = trb_next()
            bkr = trb_next()

            def mcmm(t, sub=sub, bkm=bkm, bkr=bkr, sbn=sbn):
                t.matmul(trb[bkm][:, 0:1], mrow[0:1, 128 * sub:128 * (sub + 1)],
                         one_f[:], start=True, stop=True)
                t.matmul(trb[bkr][:, 0:1], rsrow[0:1, 128 * sub:128 * (sub + 1)],
                         one_f[:], start=True, stop=True).then_inc(t_b.h, 1)
            em.emit("tensor", mcmm, waits=[(s_b, sbn), trb_free[bkm],
                                           trb_free[bkr]])
            t_b.n += 1
            tbn = t_b.n

            def cevac(v, sub=sub, bkm=bkm, bkr=bkr, tbn=tbn):
                v.tensor_copy(m_cols[:, sub:sub + 1], trb[bkm][:, 0:1])
                v.tensor_copy(rn_cols[:, sub, 0:1], trb[bkr][:, 0:1])
                v.drain()
                v.scalar_tensor_tensor(rn_cols[:, sub, 1:2], m_cols[:, sub:sub + 1],
                                       -1.0, rn_cols[:, sub, 0:1],
                                       ALU.mult, ALU.mult)
                v.drain().then_inc(v_b.h, 1)
            em.emit("vector", cevac, waits=[(t_b, tbn)])
            v_b.n += 1
            trb_free[bkm] = (v_b, v_b.n)
            trb_free[bkr] = (v_b, v_b.n)

    xsub_free = [(v_b, 0), (v_b, 0)]
    ln_read = (v_b, 0)
    xn2tm_read = (t_b, 0)
    router_read = (s_b, 0)
    pmm_prev = 0
    for sl in range(1 if stop_after == "slab1" else 4):
        tcol0 = 512 * sl
        for kc in range(8):
            k2, cpi = kc >> 2, kc & 3

            def zmm(t, k2=k2, cpi=cpi, tcol0=tcol0):
                t.matmul(ps_misc[:, 0:512], sel_sb[:, cpi, :],
                         z_recv[:, k2 * 2048 + tcol0:k2 * 2048 + tcol0 + 512],
                         start=True, stop=True).then_inc(t_b.h, 1)
            em.emit("tensor", zmm, waits=[(d_pull, 64), (v_b, v_b.n),
                                          (v_b, WBT_VB)])
            t_b.n += 1
            tbn = t_b.n

            def zn(v, kc=kc, tcol0=tcol0, tbn=tbn):
                col = kc * 2048 + tcol0
                v.reciprocal(zinv_sb[:], ps_misc[:, 0:512])
                v.drain()
                v.tensor_tensor(y_nslab[:, kc, :], y_recv[:, col:col + 512],
                                zinv_sb[:], ALU.mult)
                v.drain().then_inc(v_b.h, 1)
            em.emit("vector", zn, waits=[(t_b, tbn), (t_b, pmm_prev)])
            v_b.n += 1
        YN_VB = v_b.n
        if "ynorm" in tap_outs and sl == 0:
            def tapyn(g):
                g.dma_start(out=tap_outs["ynorm"],
                            in_=y_nslab[:].rearrange("p c t -> p (c t)")).then_inc(d_g.h, 16)
            em.emit("gpsimd", tapyn, waits=[(v_b, YN_VB)])
            d_g.n += 16

        ev_sems = []
        for oc in range(8):
            w = [(s_b, s_b.n), (v_b, YN_VB)]

            def pmm(t, oc=oc):
                for kc in range(8):
                    ins = t.matmul(ps_fm[oc % 2][:, 0:512],
                                   wp_bf[:, kc, 128 * oc:128 * (oc + 1)],
                                   y_nslab[:, kc, :],
                                   start=(kc == 0), stop=(kc == 7))
                ins.then_inc(t_b.h, 1)
            em.emit("tensor", pmm, waits=w)
            t_b.n += 1
            tbn = t_b.n
            if oc == 7:
                pmm_prev = t_b.n

            def pev(s, oc=oc, tbn=tbn):
                s.activation(attnp_fm[:, oc, :], ps_fm[oc % 2][:, 0:512], AF.Identity,
                             bias=bproj_c[:, oc:oc + 1])
                s.drain().then_inc(s_b.h, 1)
            em.emit("scalar", pev, waits=[(t_b, tbn)])
            s_b.n += 1
            ev_sems.append((s_b, s_b.n))
        fm_ln_stats(attnp_fm, ev_sems)
        row_stats_to_cols()

        for sub in range(4):
            bi2 = sub % 2

            def xl(sync, sl=sl, sub=sub, bi2=bi2):
                sync.dma_start(out=x_sub[bi2][:],
                               in_=x_res[512 * sl + 128 * sub:512 * sl + 128 * (sub + 1), :]
                               ).then_inc(d_xp[bi2].h, 16)
            em.emit("sync", xl, waits=[(bar, 5), xsub_free[bi2]])
            d_xp[bi2].n += 16
            dxn = (d_xp[bi2], d_xp[bi2].n)

            aln_last = 0
            for c in range(8):
                bk = trb_next()

                def atr(t, c=c, sub=sub, bk=bk):
                    t.matmul(trb[bk][:], attnp_fm[:, c, 128 * sub:128 * (sub + 1)],
                             ident_bf[:], start=True, stop=True).then_inc(t_b.h, 1)
                em.emit("tensor", atr, waits=[(v_b, v_b.n), (s_b, s_b.n),
                                              trb_free[bk]])
                t_b.n += 1
                tbn = t_b.n

                def aln(s, c=c, sub=sub, bk=bk, tbn=tbn):
                    if c == 7:
                        s.activation(ln_f32[:, 128 * c:128 * (c + 1)], trb[bk][:],
                                     AF.Identity, scale=rn_cols[:, sub, 0:1],
                                     bias=rn_cols[:, sub, 1:2])
                        s.drain().then_inc(s_b.h, 1)
                    else:
                        s.activation(ln_f32[:, 128 * c:128 * (c + 1)], trb[bk][:],
                                     AF.Identity, scale=rn_cols[:, sub, 0:1],
                                     bias=rn_cols[:, sub, 1:2]).then_inc(s_b.h, 1)
                em.emit("scalar", aln, waits=[(t_b, tbn), ln_read])
                s_b.n += 1
                trb_free[bk] = (s_b, s_b.n)
                aln_last = s_b.n

            def ares(v, sub=sub, aln_last=aln_last):
                v.tensor_add(x2_tm[:, sub, :], ln_f32[:], x_sub[sub % 2][:]
                             )
                v.drain().then_inc(v_b.h, 1)
            em.emit("vector", ares, waits=[(s_b, aln_last), dxn])
            v_b.n += 1
            ln_read = (v_b, v_b.n)
            xsub_free[sub % 2] = (v_b, v_b.n)

            def l2v(v, sub=sub):
                v.tensor_reduce(b_scol[:], x2_tm[:, sub, :], AX.X, ALU.add)
                v.scalar_tensor_tensor(sqb_bf[:], x2_tm[:, sub, :], 1.0,
                                       x2_tm[:, sub, :], ALU.mult, ALU.mult)
                v.tensor_reduce(b_qcol[:], sqb_bf[:], AX.X, ALU.add)
                v.tensor_scalar_mul(b_mcol[:], b_scol[:], 1.0 / D)
                v.drain()
                v.tensor_mul(b_m2[:], b_mcol[:], b_mcol[:])
                v.drain()
                v.scalar_tensor_tensor(b_vcol[:], b_qcol[:], 1.0 / D, b_m2[:],
                                       ALU.mult, ALU.subtract)
                v.drain().then_inc(v_b.h, 1)
            em.emit("vector", l2v)
            v_b.n += 1

            def l2s(s):
                s.activation(b_ln[:], b_vcol[:], AF.Ln, bias=eps_col[:, :])
                s.drain()
                s.activation(b_rs[:], b_ln[:], AF.Exp, scale=-0.5)
                s.drain().then_inc(s_b.h, 1)
            em.emit("scalar", l2s, waits=[(v_b, v_b.n)])
            s_b.n += 1

            def l2n(v):
                v.scalar_tensor_tensor(b_nm[:], b_mcol[:], -1.0, b_rs[:],
                                       ALU.mult, ALU.mult)
                v.drain().then_inc(v_b.h, 1)
            em.emit("vector", l2n, waits=[(s_b, s_b.n)])
            v_b.n += 1

            def l2no(s, sub=sub):
                s.activation(xn2_tm[:], x2_tm[:, sub, :], AF.Identity,
                             bias=b_nm[:, :], scale=b_rs[:, :])
                s.drain().then_inc(s_b.h, 1)
            em.emit("scalar", l2no, waits=[(v_b, v_b.n), xn2tm_read])
            s_b.n += 1
            sbn2 = s_b.n

            for c in range(8):
                bk = trb_next()

                def x2tr(t, c=c, bk=bk, sbn2=sbn2):
                    t.matmul(trb[bk][:], xn2_tm[:, 128 * c:128 * (c + 1)],
                             ident_bf[:], start=True, stop=True).then_inc(t_b.h, 1)
                em.emit("tensor", x2tr, waits=[(s_b, sbn2), trb_free[bk]])
                t_b.n += 1
                if c == 7:
                    xn2tm_read = (t_b, t_b.n)
                tbn3 = t_b.n

                def x2te(v, c=c, sub=sub, bk=bk, tbn3=tbn3, last=(c == 7)):
                    ins = v.tensor_copy(xn2_fm[:, c, 128 * sub:128 * (sub + 1)],
                                         trb[bk][:])
                    if last:
                        v.drain().then_inc(v_b.h, 1)
                    else:
                        ins.then_inc(v_b.h, 1)
                em.emit("vector", x2te, waits=[(t_b, tbn3)])
                v_b.n += 1
                trb_free[bk] = (v_b, v_b.n)
        XN2_VB = v_b.n

        def mm1(t):
            for ic in range(8):
                ins = t.matmul(ps_h[:, 0:512], acat_bf[:, ic, :], xn2_fm[:, ic, :],
                               start=(ic == 0), stop=(ic == 7))
            ins.then_inc(t_b.h, 1)
        em.emit("tensor", mm1, waits=[(v_b, XN2_VB)])
        t_b.n += 1
        tbn4 = t_b.n

        def hev(v, tbn4=tbn4):
            v.tensor_copy(h_fm[:], ps_h[:, 0:512])
            v.drain().then_inc(v_b.h, 1)
        em.emit("vector", hev, waits=[(t_b, tbn4)])
        v_b.n += 1
        HF_VB = v_b.n

        for sub in range(4):
            bk6 = trb_next()

            def lgmm(t, sub=sub, bk6=bk6):
                t.matmul(trb[bk6][:, 0:6], h_fm[0:6, 128 * sub:128 * (sub + 1)],
                         i6_sb[:], start=True, stop=True).then_inc(t_b.h, 1)
            em.emit("tensor", lgmm, waits=[(v_b, HF_VB), trb_free[bk6]])
            t_b.n += 1
            tbn5 = t_b.n

            def router(v, tbn5=tbn5, bk6=bk6):
                v.tensor_add(g0[:], trb[bk6][:, 0:6], rb_bc[:])
                v.drain()
                v.tensor_reduce(gmx[:], g0[:], AX.X, ALU.max)
                v.drain()
                v.tensor_scalar_mul(gmxn[:], gmx[:], -1.0)
                v.drain().then_inc(v_b.h, 1)
            em.emit("vector", router, waits=[(t_b, tbn5), router_read])
            v_b.n += 1
            trb_free[bk6] = (v_b, v_b.n)

            def rexp(s):
                s.activation(ge[:], g0[:], AF.Exp, bias=gmxn[:, :])
                s.drain().then_inc(s_b.h, 1)
            em.emit("scalar", rexp, waits=[(v_b, v_b.n)])
            s_b.n += 1
            router_read = (s_b, s_b.n)

            def top2(v):
                v.tensor_reduce(ges[:], ge[:], AX.X, ALU.add)
                v.drain()
                v.reciprocal(gesr[:], ges[:])
                v.drain()
                v.tensor_scalar(gate[:], ge[:], gesr[:, :], None, ALU.mult)
                v.drain()
                v.tensor_reduce(gmx2[:], gate[:], AX.X, ALU.max)
                v.drain()
                v.tensor_scalar(msk1[:], gate[:], gmx2[:, :], None, ALU.is_ge)
                v.drain()
                v.scalar_tensor_tensor(gg2[:], msk1[:], -1e9, gate[:], ALU.mult, ALU.add)
                v.drain()
                v.tensor_reduce(gmx3[:], gg2[:], AX.X, ALU.max)
                v.drain()
                v.tensor_scalar(msk2[:], gate[:], gmx3[:, :], None, ALU.is_ge)
                v.drain()
                v.tensor_mul(wun[:], gate[:], msk2[:])
                v.drain()
                v.tensor_reduce(wsum[:], wun[:], AX.X, ALU.add)
                v.drain()
                v.tensor_scalar(wsum2[:], wsum[:], 1e-6, None, ALU.add)
                v.drain()
                v.reciprocal(wsr[:], wsum2[:])
                v.drain()
                v.tensor_scalar(w_tm[:], wun[:], wsr[:, :], None, ALU.mult
                                )
                v.drain().then_inc(v_b.h, 1)
            em.emit("vector", top2, waits=[(s_b, s_b.n)])
            v_b.n += 1
            vbn6 = v_b.n

            bk7 = trb_next()

            def wtr(t, bk7=bk7, vbn6=vbn6):
                t.matmul(trb[bk7][0:6, 0:128], w_tm[:], ident_f[:],
                         start=True, stop=True).then_inc(t_b.h, 1)
            em.emit("tensor", wtr, waits=[(v_b, vbn6), trb_free[bk7]])
            t_b.n += 1
            wtr_last = t_b.n

            def wfev_sub(v, sub=sub, bk7=bk7, wtr_last=wtr_last):
                v.tensor_copy(w_fm7[0:6, 128 * sub:128 * (sub + 1)],
                              trb[bk7][0:6, 0:128])
                v.drain().then_inc(v_b.h, 1)
            em.emit("vector", wfev_sub, waits=[(t_b, wtr_last)])
            v_b.n += 1
            trb_free[bk7] = (v_b, v_b.n)
            if "wtm" in tap_outs and sl == 0 and sub == 0:
                def tapw(g):
                    g.dma_start(out=tap_outs["wtm"], in_=w_tm[:]).then_inc(d_g.h, 16)
                em.emit("gpsimd", tapw, waits=[(v_b, vbn6)])
                d_g.n += 16

        vbn7 = v_b.n

        def wbc(t, vbn7=vbn7):
            t.matmul(ps_misc[:, 0:512], epat_sb[:], w_fm7[:],
                     start=True, stop=True).then_inc(t_b.h, 1)
        em.emit("tensor", wbc, waits=[(v_b, vbn7), (d_w, d_w.n)])
        t_b.n += 1
        tbn8 = t_b.n

        def hwm(v, tbn8=tbn8):
            v.tensor_tensor(hw_bf[:], h_fm[:], ps_misc[:, 0:512], ALU.mult
                            )
            v.drain().then_inc(v_b.h, 1)
        em.emit("vector", hwm, waits=[(t_b, tbn8)])
        v_b.n += 1
        HW_VB = v_b.n

        ev_sems = []
        for oc in range(8):
            def fmm(t, oc=oc):
                for ic in range(8):
                    t.matmul(ps_fm[oc % 2][:, 0:512], wbt_bf[:, ic, oc, :],
                             xn2_fm[:, ic, :], start=(ic == 0), stop=False)
                t.matmul(ps_fm[oc % 2][:, 0:512],
                         bcat_bf[:, 128 * oc:128 * (oc + 1)], hw_bf[:],
                         start=False, stop=True).then_inc(t_b.h, 1)
            em.emit("tensor", fmm, waits=[(v_b, HW_VB), (s_b, s_b.n)])
            t_b.n += 1
            tbn9 = t_b.n

            def fev(s, oc=oc, tbn9=tbn9):
                s.activation(ffn_fm[:, oc, :], ps_fm[oc % 2][:, 0:512], AF.Identity,
                             bias=bbase_c[:, oc:oc + 1])
                s.drain().then_inc(s_b.h, 1)
            em.emit("scalar", fev, waits=[(t_b, tbn9)])
            s_b.n += 1
            ev_sems.append((s_b, s_b.n))
        fm_ln_stats(ffn_fm, ev_sems)
        row_stats_to_cols()

        for sub in range(4):
            fln_last = 0
            for c in range(8):
                bk = trb_next()

                def ftr(t, c=c, sub=sub, bk=bk):
                    t.matmul(trb[bk][:], ffn_fm[:, c, 128 * sub:128 * (sub + 1)],
                             ident_bf[:], start=True, stop=True).then_inc(t_b.h, 1)
                em.emit("tensor", ftr, waits=[(v_b, v_b.n), (s_b, s_b.n),
                                              trb_free[bk]])
                t_b.n += 1
                tbn10 = t_b.n

                def fln(s, c=c, sub=sub, bk=bk, tbn10=tbn10):
                    if c == 7:
                        s.activation(ln_f32[:, 128 * c:128 * (c + 1)], trb[bk][:],
                                     AF.Identity, scale=rn_cols[:, sub, 0:1],
                                     bias=rn_cols[:, sub, 1:2])
                        s.drain().then_inc(s_b.h, 1)
                    else:
                        s.activation(ln_f32[:, 128 * c:128 * (c + 1)], trb[bk][:],
                                     AF.Identity, scale=rn_cols[:, sub, 0:1],
                                     bias=rn_cols[:, sub, 1:2]).then_inc(s_b.h, 1)
                em.emit("scalar", fln, waits=[(t_b, tbn10), ln_read])
                s_b.n += 1
                trb_free[bk] = (s_b, s_b.n)
                fln_last = s_b.n

            def fres(v, sub=sub, fln_last=fln_last):
                v.tensor_add(out_tm[sub % 2][:], ln_f32[:], x2_tm[:, sub, :]
                             )
                v.drain().then_inc(v_b.h, 1)
            em.emit("vector", fres, waits=[(s_b, fln_last),
                                           (d_op[sub % 2], d_op[sub % 2].n)])
            v_b.n += 1
            ln_read = (v_b, v_b.n)
            vbn12 = v_b.n

            def ow(sync, sl=sl, sub=sub, vbn12=vbn12):
                sync.dma_start(
                    out=out_half[512 * sl + 128 * sub:512 * sl + 128 * (sub + 1), :],
                    in_=out_tm[sub % 2][:]).then_inc(d_op[sub % 2].h, 16)
            em.emit("sync", ow, waits=[(v_b, vbn12)])
            d_op[sub % 2].n += 16

        if "attnp" in tap_outs and sl == 0:
            def tapap(g):
                g.dma_start(out=tap_outs["attnp"],
                            in_=attnp_fm[:].rearrange("p c t -> p (c t)")).then_inc(d_g.h, 16)
                g.dma_start(out=tap_outs["x2"],
                            in_=x2_tm[:].rearrange("p c t -> p (c t)")).then_inc(d_g.h, 16)
                g.dma_start(out=tap_outs["hfm"], in_=h_fm[:]).then_inc(d_g.h, 16)
                g.dma_start(out=tap_outs["hw"], in_=hw_bf[:]).then_inc(d_g.h, 16)
                g.dma_start(out=tap_outs["ffn"],
                            in_=ffn_fm[:].rearrange("p c t -> p (c t)")).then_inc(d_g.h, 16)
            em.emit("gpsimd", tapap, waits=[(v_b, v_b.n), (s_b, s_b.n)])
            d_g.n += 5 * 16

    def fin(sync):
        sync.nop()
    em.emit("sync", fin, waits=[(d_op[0], d_op[0].n), (d_op[1], d_op[1].n)])
    if tap_outs:
        def fing(g):
            g.nop()
        em.emit("gpsimd", fing, waits=[(d_g, d_g.n)])

    replay_block(em)
    pctxB.close()
    ctxB.close()
    sctx.close()
    ctx0.close()
    return nc


def _prep_inputs(inputs):
    x = np.ascontiguousarray(np.asarray(inputs["L0_attn_x"], dtype=np.float32))
    w_attn = np.asarray(inputs["L0_attn_w_attn"], dtype=np.float32)
    b_attn = np.asarray(inputs["L0_attn_b_attn"], dtype=np.float32)
    w_proj = np.asarray(inputs["L0_attn_w_proj"], dtype=np.float32)
    b_proj = np.asarray(inputs["L0_attn_b_proj"], dtype=np.float32)
    base_W = np.asarray(inputs["L0_ffn_base_W"], dtype=np.float32)
    base_b = np.asarray(inputs["L0_ffn_base_b"], dtype=np.float32)
    shared_A = np.asarray(inputs["L0_ffn_shared_A"], dtype=np.float32)
    shared_B = np.asarray(inputs["L0_ffn_shared_B"], dtype=np.float32)
    routed_A = np.asarray(inputs["L0_ffn_routed_A"], dtype=np.float32)
    routed_B = np.asarray(inputs["L0_ffn_routed_B"], dtype=np.float32)
    router_W = np.asarray(inputs["L0_ffn_router_W"], dtype=np.float32)
    router_b = np.asarray(inputs["L0_ffn_router_b"], dtype=np.float32)

    cst = _HOST_CONSTS
    acat = np.zeros((D, 128), dtype=np.float32)
    acat[:, 0:6] = router_W.T
    acat[:, 16:32] = shared_A.T
    acat[:, 32:128] = routed_A.reshape(96, D).T
    bcat = np.zeros((128, D), dtype=np.float32)
    bcat[16:32, :] = shared_B.T
    bcat[32:128, :] = np.transpose(routed_B, (0, 2, 1)).reshape(96, D)

    common = dict(
        wb_h=np.ascontiguousarray(base_W),
        acat_h=np.ascontiguousarray(acat),
        bcat_h=np.ascontiguousarray(bcat),
        rb_h=np.ascontiguousarray(router_b.reshape(1, 6)),
        bbase_h=np.ascontiguousarray(base_b.reshape(8, 128).T),
        mask5_h=np.ascontiguousarray(
            cst["mask5"].transpose(1, 0, 2).reshape(128, 5 * 512)),
        identb_h=cst["ident_bf"],
        identf_h=cst["ident_f32"],
        onesr_h=np.ones((1, 512), dtype=np.float32),
        i6_h=cst["i6_f32"],
        onesb_h=cst["ones_bf"],
        onef_h=cst["one_f32"],
        eps_h=cst["eps_col"],
        epat_h=cst["e_pat"],
        sel_h=np.ascontiguousarray(
            cst["sel_pat"].transpose(1, 0, 2).reshape(8, 4 * 128)),
        magic_h=np.random.default_rng().integers(
            -2**31, 2**31, size=(1, 128), dtype=np.int64).astype(np.int32),
    )

    in_maps = []
    for c in range(8):
        j, k = c >> 1, c & 1
        heads = [2 * i + k for i in range(NSLOT)]
        forder = []
        for k2 in range(2):
            for cp in range(4):
                for half in range(2):
                    g = 4 * cp + 2 * half + k2
                    forder.extend(range(g * HD, (g + 1) * HD))
        forder = np.array(forder)

        def slot_cols(off):
            cols = []
            for s in heads:
                cols.extend(range(off + s * HD, off + (s + 1) * HD))
            return np.array(cols)

        m = dict(common)
        m["x_seq"] = np.ascontiguousarray(x[j])
        m["x_res"] = np.ascontiguousarray(x[j, 2048 * k:2048 * (k + 1)])
        m["wq_h"] = np.ascontiguousarray(w_attn[:, slot_cols(0)])
        m["wk_h"] = np.ascontiguousarray(w_attn[:, slot_cols(D)])
        m["wv_h"] = np.ascontiguousarray(w_attn[:, slot_cols(2 * D)])
        m["wp_h"] = np.ascontiguousarray(w_proj[forder, :])
        m["bq_h"] = np.ascontiguousarray(b_attn[slot_cols(0)].reshape(4, 128).T)
        m["bv_h"] = np.ascontiguousarray(b_attn[slot_cols(2 * D)].reshape(1, 512))
        m["bproj_h"] = np.ascontiguousarray(b_proj.reshape(8, 128).T)
        m["bias_tab_h"] = build_bias_tab(k).reshape(128, NSLOT * 64)
        in_maps.append(m)
    return in_maps


_NC_CACHE = {}


def kernel(**inputs):
    if "main" not in _NC_CACHE:
        _NC_CACHE["main"] = build_program()
    nc = _NC_CACHE["main"]
    in_maps = _prep_inputs(inputs)
    res = run_bass_kernel_spmd(nc, in_maps, core_ids=list(range(8)))
    out = np.empty((4, T, D), dtype=np.float32)
    for c in range(8):
        j, k = c >> 1, c & 1
        out[j, 2048 * k:2048 * (k + 1)] = res.results[c]["out_half"]
    return out
